# version 17 (final)
# baseline (speedup 1.0000x reference)
import numpy as np
import ml_dtypes
import concourse.bass as bass
import concourse.mybir as mybir
from concourse.bass_utils import run_bass_kernel_spmd

F32 = mybir.dt.float32
BF16 = mybir.dt.bfloat16
AF = mybir.ActivationFunctionType
ALU = mybir.AluOpType
AX = mybir.AxisListType

P = 128
D = 1024
T = 8192
NTILE = 16
NBLK = 8
NEG_BIG = 3.0e38
EPS = 1e-5


def build():
    nc = bass.Bass("TRN2", target_bir_lowering=False)

    xq = nc.dram_tensor("xq", [2048, D], F32, kind="ExternalInput").ap()
    wq_d = nc.dram_tensor("wq", [D, D], F32, kind="ExternalInput").ap()
    wk_d = nc.dram_tensor("wk", [D, D], F32, kind="ExternalInput").ap()
    wv_d = nc.dram_tensor("wv", [D, D], F32, kind="ExternalInput").ap()
    wt_d = nc.dram_tensor("wt", [D, D], F32, kind="ExternalInput").ap()
    bq_in = nc.dram_tensor("bq", [P, 8], F32, kind="ExternalInput").ap()
    bk_in = nc.dram_tensor("bk", [P, 8], F32, kind="ExternalInput").ap()
    bv_in = nc.dram_tensor("bv", [1, D], F32, kind="ExternalInput").ap()
    fb_in = nc.dram_tensor("fb", [1, D], F32, kind="ExternalInput").ap()
    al_in = nc.dram_tensor("alpha", [1, 1], F32, kind="ExternalInput").ap()
    maska_in = nc.dram_tensor("maska", [P, D], BF16, kind="ExternalInput").ap()
    ident_in = nc.dram_tensor("ident", [P, P], BF16, kind="ExternalInput").ap()
    onescol_in = nc.dram_tensor("onescol", [1, P], BF16, kind="ExternalInput").ap()

    y = nc.dram_tensor("y", [2048, D], F32, kind="ExternalOutput").ap()

    cc_in = nc.dram_tensor("cc_in", [1, 2], F32)
    cc_out = nc.dram_tensor("cc_out", [4, 2], F32)
    ccKV_in = [nc.dram_tensor("cckv_in%d" % i, [2048, 512], BF16) for i in range(4)]
    ccKV_out = [nc.dram_tensor("cckv_out%d" % i, [8192, 512], BF16) for i in range(4)]

    SBUF_BASE = 16512
    nc.alloc_sbuf_tensor("_spacer", [P, (SBUF_BASE + 212608 - 16488) // 4], F32)

    def sbat(name, shape, dtype, off):
        return nc.alloc_sbuf_tensor_at(name, shape, dtype, offset=SBUF_BASE + off)

    O = sbat("O", [P, 16 * D], F32, 0)
    QF = sbat("QF", [P, 8 * 2048], BF16, 65536)
    WKB = sbat("WKB", [P, 8 * D], BF16, 98304)
    KPB_lo = sbat("KPB_lo", [P, 8 * D], BF16, 98304)
    WVB = sbat("WVB", [P, 8 * D], BF16, 114688)
    VB1 = sbat("VB1", [P, 8 * D], BF16, 114688)
    WQB = sbat("WQB", [P, 8 * D], BF16, 131072)
    KPB_hi = sbat("KPB_hi", [P, 8 * D], BF16, 131072)
    WTB = sbat("WTB", [P, 8 * D], BF16, 131072)
    KPA = sbat("KPA", [P, 16 * D], BF16, 163840)
    FST = sbat("FST", [P, 8 * D], BF16, 147456)
    VB0 = sbat("VB0", [P, 8 * D], BF16, 147456)
    LNQF = sbat("LNQF", [P, 8 * D], BF16, 163840)
    XB = [sbat("XB0", [P, D], F32, 188416),
          sbat("XB1", [P, D], F32, 192512),
          sbat("XB2", [P, D], F32, 180224),
          sbat("XB3", [P, D], F32, 184320)]
    VSTG = sbat("VSTG", [P, 4 * D], BF16, 188416)
    LNT = [sbat("LNT0", [P, D], BF16, 196608),
           sbat("LNT1", [P, D], BF16, 202752)]
    ETS1 = sbat("ETS1", [P, D], BF16, 198656)
    MASKA = sbat("MASKA", [P, D], BF16, 200704)
    EB = sbat("EB", [P, D], BF16, 204800)
    ETS0 = sbat("ETS0", [P, D], BF16, 206848)
    ETS = [ETS0, ETS1]
    WST = [sbat("WST0", [P, D], F32, 147456),
           sbat("WST1", [P, D], F32, 151552)]
    OST = [sbat("OST0", [P, D], F32, 155648),
           sbat("OST1", [P, D], F32, 159744)]
    GTA = sbat("GTA", [P, D], F32, 163840)
    GTB = sbat("GTB", [P, D], F32, 167936)
    STIL = sbat("STIL", [P, D], BF16, 172032)
    FBROW = sbat("FBROW", [1, D], BF16, 206848)
    BSROW = sbat("BSROW", [1, D], BF16, 204800)
    LN2F = sbat("LN2F", [P, 8 * P], BF16, 180224)
    SROW = sbat("SROW", [1, P], F32, 200704)
    ABSA = sbat("ABSA", [P, 8], F32, 201216)
    DLT = sbat("DLT", [1, 1], F32, 201248)
    DLB = sbat("DLB", [P, 1], F32, 201280)
    NDLB = sbat("NDLB", [P, 1], F32, 201312)
    ALV = sbat("ALV", [1, 1], F32, 201344)
    T1 = sbat("T1", [1, 1], F32, 201376)
    SCP = sbat("SCP", [1, 2], F32, 201408)
    G8 = sbat("G8", [1, 8], F32, 201440)
    MG = sbat("MG", [1, 1], F32, 201472)
    NMG2 = sbat("NMG2", [1, 1], F32, 201504)
    ER4 = sbat("ER4", [1, 4], F32, 201536)
    ZR4 = sbat("ZR4", [1, 4], F32, 201568)
    ZS = sbat("ZS", [1, 1], F32, 201600)
    IVZ = sbat("IVZ", [1, 1], F32, 201632)
    PAIR = sbat("PAIR", [1, 2], F32, 201664)
    FB2 = sbat("FB2", [P, 2], F32, 201696)
    NMINB = sbat("NMINB", [P, 1], BF16, 201728)
    F2 = sbat("F2", [P, 2], F32, 201760)
    AHB = sbat("AHB", [P, 1], F32, 201792)
    NMIN = sbat("NMIN", [P, 1], F32, 201824)
    ZC = sbat("ZC", [P, 1], F32, 201856)
    EXS = sbat("EXS", [P, NTILE], F32, 201888)
    BVROW = sbat("BVROW", [1, D], BF16, 208896)
    ONESR = sbat("ONESR", [1, P], F32, 210944)
    IDENT = sbat("IDENT", [P, P], BF16, 211456)
    ONESC = sbat("ONESC", [1, P], BF16, 211712)
    BQ = sbat("BQ", [P, 8], F32, 211968)
    BK = sbat("BK", [P, 8], F32, 212000)
    ONESF = sbat("ONESF", [P, 1], F32, 212032)
    EPSC = sbat("EPSC", [P, 1], F32, 212064)
    NMRUN = sbat("NMRUN", [P, NTILE], F32, 212096)
    LRUN = sbat("LRUN", [P, NTILE], F32, 212160)
    BN6 = sbat("BN6", [P, 24], F32, 212224)
    MV = sbat("MV", [P, 4], F32, 212320)
    LNV = sbat("LNV", [P, 2], F32, 212352)
    RSTD = sbat("RSTD", [P, 2], F32, 212384)
    NMR = sbat("NMR", [P, 2], F32, 212416)
    NMAX = sbat("NMAX", [P, 4], F32, 212448)
    NMS = sbat("NMS", [P, 4], F32, 212480)
    DSC = sbat("DSC", [P, 4], F32, 212512)
    ALS = sbat("ALS", [P, 4], F32, 212544)
    ACC = sbat("ACC", [P, 4], F32, 212576)

    VBUF = [VB0, VB1]

    PS_S = [nc.alloc_psum_tensor("ps_sa", [P, D], F32),
            nc.alloc_psum_tensor("ps_sb", [P, D], F32)]
    PS_AV = nc.alloc_psum_tensor("ps_av", [P, D], F32)
    PS_PR = nc.alloc_psum_tensor("ps_pr", [P, 512], F32)
    PS_PR2 = nc.alloc_psum_tensor("ps_pr2", [P, 512], F32)

    ops = {"t": [], "v": [], "s": [], "g": [], "y": []}

    class Ctr:
        def __init__(self):
            self.v = 0

        def inc(self, n=1):
            self.v += n
            return self.v

    cT, cS, cV, cG = Ctr(), Ctr(), Ctr(), Ctr()
    cX = [Ctr(), Ctr(), Ctr(), Ctr()]
    cST = [Ctr(), Ctr()]
    cKL = [Ctr(), Ctr()]
    cVL = [Ctr(), Ctr()]
    YCTR = {"x0": cX[0], "x1": cX[1], "x2": cX[2], "x3": cX[3],
            "st0": cST[0], "st1": cST[1],
            "kl0": cKL[0], "kl1": cKL[1], "vl0": cVL[0], "vl1": cVL[1]}

    def wt(eng, sem, val):
        if val and val > 0:
            ops[eng].append(lambda e, S: e.wait_ge(S[sem], val))

    def top(fn, inc=False):
        if inc:
            def f(e, S):
                fn(e).then_inc(S["t"], 1)
            ops["t"].append(f)
            return cT.inc()
        ops["t"].append(lambda e, S: fn(e))
        return None

    def vop(fn):
        def f(e, S):
            fn(e).then_inc(S["v"], 1)
        ops["v"].append(f)
        return cV.inc()

    def sop(fn):
        def f(e, S):
            fn(e).then_inc(S["s"], 1)
        ops["s"].append(f)
        return cS.inc()

    def gdma(fn):
        def f(e, S):
            fn(e).then_inc(S["g"], 16)
        ops["g"].append(f)
        return cG.inc(16)

    def ydma(fn, sem):
        def f(e, S):
            fn(e).then_inc(S[sem], 16)
        ops["y"].append(f)
        return YCTR[sem].inc(16)

    def gbar():
        val = cG.v
        ops["g"].append(lambda e, S: e.wait_ge(S["g"], val))

    gdma(lambda e: e.dma_start(out=IDENT[:, :], in_=ident_in))
    gdma(lambda e: e.dma_start(out=MASKA[:, :], in_=maska_in))
    gdma(lambda e: e.dma_start(out=ONESC[:, :], in_=onescol_in))
    gdma(lambda e: e.dma_start(out=BQ[:, :], in_=bq_in))
    gdma(lambda e: e.dma_start(out=BK[:, :], in_=bk_in))
    g_smalls = gdma(lambda e: e.dma_start(out=BVROW[:, :], in_=bv_in))

    def wload(dst, src):
        return gdma(lambda e: e.dma_start(
            out=dst[:, :].rearrange("p (e mc) -> p e mc", e=8),
            in_=src.rearrange("(e p) mc -> p e mc", p=P)))

    gbar()
    g_wq = wload(WQB, wq_d)
    g_wk = wload(WKB, wk_d)
    g_wv = wload(WVB, wv_d)
    g_aux_all = g_wv
    gbar()

    vop(lambda e: e.memset(ONESF[:, :], 1.0))
    vop(lambda e: e.memset(ONESR[:, :], 1.0))
    vop(lambda e: e.memset(EPSC[:, :], EPS))
    vop(lambda e: e.memset(NMRUN[:, :], NEG_BIG))
    vop(lambda e: e.memset(LRUN[:, :], 0.0))
    for j in range(4):
        vop(lambda e, j=j: e.memset(O[:, j * 4096:(j + 1) * 4096], 0.0))

    lnt_free_T = [0, 0]
    apply_guard_S = [0, 0]
    PRB = [PS_PR, PS_PR2]
    pr_n = [0]
    pr_done = [0, 0]
    DSLOT = [(PS_PR, PS_PR2),
             (PS_S[0][:, 0:512], PS_S[0][:, 512:1024]),
             (PS_S[1][:, 0:512], PS_S[1][:, 512:1024])]
    ds_n = [0]
    ds_done = [0, 0, 0]
    xb_free = [0, 0, 0, 0]

    def xload(src_rows, xpar):
        wt("y", "s", xb_free[xpar])
        return ydma(lambda e: e.dma_start(out=XB[xpar][:, :], in_=src_rows),
                    "x%d" % xpar)

    def ln_task(par, x_wait, xpar=None):
        if xpar is None:
            xpar = par
        wt("v", "s", apply_guard_S[par])
        wt("v", "x%d" % xpar, cX[xpar].v)
        vop(lambda e: e.bn_stats(BN6[:, par * 12:par * 12 + 6], XB[xpar][:, 0:512]))
        v_bn = vop(lambda e: e.bn_stats(BN6[:, par * 12 + 6:par * 12 + 12],
                                        XB[xpar][:, 512:1024]))
        wt("v", "v", v_bn)
        v_ag = vop(lambda e: e.bn_aggr(MV[:, par * 2:par * 2 + 2],
                                       BN6[:, par * 12:par * 12 + 12]))
        wt("s", "v", v_ag)
        s_ln = sop(lambda e: e.activation(LNV[:, par:par + 1], MV[:, par * 2 + 1:par * 2 + 2],
                                          AF.Ln, bias=EPSC[:, 0:1]))
        wt("s", "s", s_ln)
        s_rs = sop(lambda e: e.activation(RSTD[:, par:par + 1], LNV[:, par:par + 1],
                                          AF.Exp, scale=-0.5))
        wt("v", "s", s_rs)
        v_nm = vop(lambda e: e.tensor_scalar(NMR[:, par:par + 1], MV[:, par * 2:par * 2 + 1],
                                             RSTD[:, par:par + 1], -1.0,
                                             ALU.mult, ALU.mult))
        wt("s", "v", v_nm)
        wt("s", "t", lnt_free_T[par])
        s_ap = sop(lambda e, xpar=xpar: e.activation(
            LNT[par][:, :], XB[xpar][:, :], AF.Identity,
            scale=RSTD[:, par:par + 1], bias=NMR[:, par:par + 1]))
        apply_guard_S[par] = s_ap
        xb_free[xpar] = s_ap
        return s_ap

    def ln_transpose(par, s_apply, dst, col_fn):
        for half in range(2):
            slot = pr_n[0] % 2
            pr_n[0] += 1
            prb = PRB[slot]
            wt("t", "s", s_apply)
            wt("t", "s", pr_done[slot])
            t_grp = None
            for cc in range(4):
                c = half * 4 + cc
                t_grp = top(lambda e, c=c, cc=cc, prb=prb: e.matmul(
                    prb[:, cc * P:(cc + 1) * P],
                    LNT[par][:, c * P:(c + 1) * P], IDENT[:, :],
                    start=True, stop=True), inc=(cc == 3))
            wt("s", "t", t_grp)
            for cc in range(4):
                c = half * 4 + cc
                col = col_fn(c)
                if cc == 3:
                    pr_done[slot] = sop(lambda e, col=col, cc=cc, prb=prb: e.copy(
                        dst[:, col:col + P], prb[:, cc * P:(cc + 1) * P]))
                else:
                    sop(lambda e, col=col, cc=cc, prb=prb: e.copy(
                        dst[:, col:col + P], prb[:, cc * P:(cc + 1) * P]))
        lnt_free_T[par] = cT.v
        return pr_done[(pr_n[0] - 1) % 2]

    def proj_group(lhsT_fn, rhs_fn, bias_mm, evac_fn, wait_g=0, wait_s=0):
        slot = pr_n[0] % 2
        pr_n[0] += 1
        prb = PRB[slot]
        wt("t", "s", pr_done[slot])
        if wait_s:
            wt("t", "s", wait_s)
        if wait_g:
            wt("t", "g", wait_g)
        t_grp = None
        for c in range(8):
            last = (c == 7) and bias_mm is None
            t_grp = top(lambda e, c=c, prb=prb: e.matmul(
                prb[:, 0:512], lhsT_fn(c), rhs_fn(c),
                start=(c == 0), stop=(c == 7 and bias_mm is None)), inc=last)
        if bias_mm is not None:
            t_grp = top(lambda e, prb=prb: bias_mm(e, prb), inc=True)
        wt("s", "t", t_grp)
        pr_done[slot] = sop(lambda e, prb=prb: evac_fn(e, prb))
        return pr_done[slot]

    def dual_group(lhsT_fn, rhs_fn, bias_mm, evac_fn, wait_g=0, wait_s=0):
        slot = ds_n[0] % 3
        ds_n[0] += 1
        ba, bb = DSLOT[slot]
        wt("t", "s", ds_done[slot])
        if slot == 0:
            wt("t", "s", pr_done[0])
            wt("t", "s", pr_done[1])
        if wait_s:
            wt("t", "s", wait_s)
        if wait_g:
            wt("t", "g", wait_g)
        t_grp = None
        for c in range(8):
            for h in range(2):
                prb = ba if h == 0 else bb
                last = (c == 7 and h == 1) and bias_mm is None
                t_grp = top(lambda e, c=c, h=h, prb=prb: e.matmul(
                    prb[:, 0:512], lhsT_fn(c), rhs_fn(c, h),
                    start=(c == 0), stop=(c == 7 and bias_mm is None)), inc=last)
        if bias_mm is not None:
            top(lambda e, ba=ba: bias_mm(e, ba, 0), inc=False)
            t_grp = top(lambda e, bb=bb: bias_mm(e, bb, 1), inc=True)
        wt("s", "t", t_grp)
        sop(lambda e, ba=ba: evac_fn(e, ba, 0))
        ds_done[slot] = sop(lambda e, bb=bb: evac_fn(e, bb, 1))
        if slot == 0:
            pr_done[0] = ds_done[slot]
            pr_done[1] = ds_done[slot]
        return ds_done[slot]

    def pr_small(mm_fn, evac_fn, wv=0, ws=0):
        slot = pr_n[0] % 2
        pr_n[0] += 1
        prb = PRB[slot]
        wt("t", "s", pr_done[slot])
        if wv:
            wt("t", "v", wv)
        if ws:
            wt("t", "s", ws)
        t_b = top(lambda e, prb=prb: mm_fn(e, prb), inc=True)
        wt("s", "t", t_b)
        pr_done[slot] = sop(lambda e, prb=prb: evac_fn(e, prb))
        return pr_done[slot]

    wt("t", "g", g_smalls)
    q_done_qq = [0, 0, 0, 0]
    pre_done = {}

    def pre_A(t):
        par = t % 2
        xpar = t % 4
        xw = xload(xq[t * P:(t + 1) * P, :], xpar)
        return (par, ln_task(par, xw, xpar), t)

    def pre_B(st_):
        par, s_ap, t = st_
        return t, ln_transpose(par, s_ap, LNQF,
                               lambda c, t=t: 1024 * c + P * (t % 8))

    def q_proj(qq, s_lnq):
        for e_ in range(8):
            q_done_qq[qq] = proj_group(
                lambda c, e_=e_: WQB[:, c * D + e_ * P: c * D + (e_ + 1) * P],
                lambda c, qq=qq: LNQF[:, c * 1024 + (qq % 2) * 512:
                                      c * 1024 + (qq % 2) * 512 + 512],
                None,
                lambda e, prb, e_=e_, qq=qq: e.activation(
                    QF[:, e_ * 2048 + qq * 512: e_ * 2048 + (qq + 1) * 512],
                    prb[:, 0:512], AF.Identity, bias=BQ[:, e_:e_ + 1]),
                wait_g=g_aux_all, wait_s=s_lnq)

    stq = []
    xw8 = {}
    for t in range(8):
        stq.append(pre_A(t))
        if t >= 4:
            u = t + 4
            xw8[u] = xload(xq[u * P:(u + 1) * P, :], u % 4)
        if len(stq) >= 2:
            tt, s_l = pre_B(stq.pop(0))
            pre_done[tt] = s_l
            if tt == 3:
                q_proj(0, s_l)
    while stq:
        tt, s_l = pre_B(stq.pop(0))
        pre_done[tt] = s_l
    q_proj(1, pre_done[7])

    k01_done = 0
    for e_ in range(8):
        k01_done = dual_group(
            lambda c, e_=e_: WKB[:, c * D + e_ * P: c * D + (e_ + 1) * P],
            lambda c, h: LNQF[:, c * 1024 + h * 512: c * 1024 + h * 512 + 512],
            None,
            lambda e, prb, h, e_=e_: e.activation(
                QF[:, e_ * 2048 + 1024 + h * 512: e_ * 2048 + 1536 + h * 512],
                prb[:, 0:512], AF.Identity, bias=BK[:, e_:e_ + 1]),
            wait_g=g_aux_all, wait_s=pre_done[7])
    v01_done = 0
    for m in range(8):
        v01_done = dual_group(
            lambda c, m=m: LNQF[:, c * 1024 + m * P: c * 1024 + (m + 1) * P],
            lambda c, h: WVB[:, c * D + h * 512: c * D + h * 512 + 512],
            lambda e, prb, h: e.matmul(
                prb[:, 0:512], ONESC[:, :],
                BVROW[0:1, h * 512:(h + 1) * 512], start=False, stop=True),
            lambda e, prb, h, m=m: e.copy(
                FST[:, m * D + h * 512: m * D + h * 512 + 512], prb[:, 0:512]),
            wait_g=g_aux_all, wait_s=pre_done[m])

    wt("y", "s", k01_done)
    yKout = [0, 0, 0, 0]
    yVout = [0, 0, 0, 0]
    for G in range(2):
        wt("y", "st0", cST[0].v)
        yKout[G] = ydma(lambda e, G=G: e.dma_start(
            out=ccKV_in[G].ap()[0:1024, :].rearrange("(p e) q -> p e q", p=P),
            in_=QF[:, :].rearrange("p (e q) -> p e q", e=8)[
                :, :, 1024 + G * 512: 1536 + G * 512]), "st0")
    wt("y", "s", v01_done)
    for G in range(2):
        wt("y", "st1", cST[1].v)
        yVout[G] = ydma(lambda e, G=G: e.dma_start(
            out=ccKV_in[G].ap()[1024:2048, :].rearrange(
                "(q t e2) c -> t q (e2 c)", q=4, e2=2),
            in_=FST[:, G * 4096:(G + 1) * 4096].rearrange(
                "t (q emb) -> t q emb", q=4)), "st1")

    cc_n = [0]
    ccKV_idx = [None, None, None, None]

    def emit_cc(inb, outb, ywait=None):
        wt("g", "st0", cST[0].v)
        wt("g", "st1", cST[1].v)
        key = "cc%d" % cc_n[0]
        cc_n[0] += 1

        def f(e, S):
            e.collective_compute(
                "AllGather", ALU.bypass,
                replica_groups=[[0, 1, 2, 3], [4, 5, 6, 7]],
                ins=[inb.ap()], outs=[outb.ap()]).then_inc(S[key], 1)
        ops["g"].append(f)
        return key

    for G in range(2):
        ccKV_idx[G] = emit_cc(ccKV_in[G], ccKV_out[G])

    def pre_A2(t):
        par = t % 2
        xpar = t % 4
        xw = xw8[t] if t in xw8 else xload(xq[t * P:(t + 1) * P, :], xpar)
        return (par, ln_task(par, xw, xpar), t)

    stq = []
    for t in range(8, 16):
        stq.append(pre_A2(t))
        if len(stq) >= 2:
            tt, s_l = pre_B(stq.pop(0))
            pre_done[tt] = s_l
            if tt == 11:
                wt("s", "st0", cST[0].v)
                q_proj(2, s_l)
            if tt == 13:
                vg2_done = 0
                for m in range(8, 12):
                    vg2_done = dual_group(
                        lambda c, m=m: LNQF[:, c * 1024 + (m % 8) * P:
                                            c * 1024 + (m % 8 + 1) * P],
                        lambda c, h: WVB[:, c * D + h * 512: c * D + h * 512 + 512],
                        lambda e, prb, h: e.matmul(
                            prb[:, 0:512], ONESC[:, :],
                            BVROW[0:1, h * 512:(h + 1) * 512], start=False, stop=True),
                        lambda e, prb, h, m=m: e.copy(
                            VSTG[:, (m - 8) * D + h * 512: (m - 8) * D + h * 512 + 512],
                            prb[:, 0:512]),
                        wait_g=g_aux_all, wait_s=pre_done[m])
                wt("y", "s", vg2_done)
                wt("y", "st1", cST[1].v)
                yVout[2] = ydma(lambda e: e.dma_start(
                    out=ccKV_in[2].ap()[1024:2048, :].rearrange(
                        "(q t e2) c -> t q (e2 c)", q=4, e2=2),
                    in_=VSTG[:, :].rearrange("t (q emb) -> t q emb", q=4)), "st1")
    while stq:
        tt, s_l = pre_B(stq.pop(0))
        pre_done[tt] = s_l

    wt("s", "st1", cST[1].v)
    k23_done = 0
    for e_ in range(8):
        k23_done = dual_group(
            lambda c, e_=e_: WKB[:, c * D + e_ * P: c * D + (e_ + 1) * P],
            lambda c, h: LNQF[:, c * 1024 + h * 512: c * 1024 + h * 512 + 512],
            None,
            lambda e, prb, h, e_=e_: e.activation(
                FST[:, e_ * D + h * 512: e_ * D + h * 512 + 512],
                prb[:, 0:512], AF.Identity, bias=BK[:, e_:e_ + 1]),
            wait_g=g_aux_all, wait_s=pre_done[15])
    t_after_K23 = cT.v
    q_proj(3, pre_done[15])
    t_after_Q23 = cT.v
    wt("y", "s", k23_done)
    for G in range(2, 4):
        wt("y", "st0", cST[0].v)
        yKout[G] = ydma(lambda e, G=G: e.dma_start(
            out=ccKV_in[G].ap()[0:1024, :].rearrange("(p e) q -> p e q", p=P),
            in_=FST[:, :].rearrange("p (e q) -> p e q", e=8)[
                :, :, (G - 2) * 512: (G - 1) * 512]), "st0")
    ccKV_idx[2] = emit_cc(ccKV_in[2], ccKV_out[2])

    wt("s", "st1", cST[1].v)
    vg3_done = 0
    for m in range(12, 16):
        vg3_done = dual_group(
            lambda c, m=m: LNQF[:, c * 1024 + (m % 8) * P:
                                c * 1024 + (m % 8 + 1) * P],
            lambda c, h: WVB[:, c * D + h * 512: c * D + h * 512 + 512],
            lambda e, prb, h: e.matmul(
                prb[:, 0:512], ONESC[:, :],
                BVROW[0:1, h * 512:(h + 1) * 512], start=False, stop=True),
            lambda e, prb, h, m=m: e.copy(
                VSTG[:, (m - 12) * D + h * 512: (m - 12) * D + h * 512 + 512],
                prb[:, 0:512]),
            wait_g=g_aux_all, wait_s=pre_done[m])
    t_after_V3 = cT.v
    wt("y", "s", vg3_done)
    wt("y", "st1", cST[1].v)
    yVout[3] = ydma(lambda e: e.dma_start(
        out=ccKV_in[3].ap()[1024:2048, :].rearrange(
            "(q t e2) c -> t q (e2 c)", q=4, e2=2),
        in_=VSTG[:, :].rearrange("t (q emb) -> t q emb", q=4)), "st1")
    ccKV_idx[3] = emit_cc(ccKV_in[3], ccKV_out[3])
    pre_end_S = cS.v

    def land_K_pair(G, wait_half=(0, 0)):
        wt("y", ccKV_idx[G], 1)
        for half in range(2):
            wt("y", "t", wait_half[half])
            for rr in range(2):
                r = half * 2 + rr
                if G % 2 == 0:
                    buf = KPB_lo if half == 0 else KPB_hi
                    dst = buf[:, rr * 4096:(rr + 1) * 4096]
                else:
                    dst = KPA[:, r * 4096:(r + 1) * 4096]
                src = ccKV_out[G].ap()[r * 2048:r * 2048 + 1024, :].rearrange(
                    "(p e) q -> p (e q)", p=P)
                ydma(lambda e, dst=dst, src=src: e.dma_start(out=dst, in_=src),
                     "kl%d" % (G % 2))
        return cKL[G % 2].v

    def land_V(pb):
        G, b = pb // 2, pb % 2
        buf = VBUF[pb % 2]
        wt("g", ccKV_idx[G], 1)
        for r in range(4):
            def f(e, S, G=G, b=b, r=r, buf=buf, pb=pb):
                base = r * 2048 + 1024 + b * 512
                e.dma_start(
                    out=buf[:, r * 2048:(r + 1) * 2048].rearrange(
                        "t (hp emb) -> t hp emb", hp=2),
                    in_=ccKV_out[G].ap()[base: base + 512, :].rearrange(
                        "(hp t e2) c -> t hp (e2 c)", hp=2, e2=2)
                ).then_inc(S["vl%d" % (pb % 2)], 16)
            ops["g"].append(f)
            cVL[pb % 2].inc(16)
        return cVL[pb % 2].v

    land_K_pair(0, (t_after_K23, t_after_Q23))
    wt("g", "st0", cST[0].v)
    wt("g", "st1", cST[1].v)
    land_V(0)
    wt("y", "t", t_after_V3)
    wt("y", "s", xb_free[2])
    wt("y", "s", xb_free[3])
    wt("y", "st1", cST[1].v)
    land_K_pair(1)
    wt("g", "t", t_after_V3)
    land_V(1)

    exp_done_S = {}
    v3_done_V = {}
    evac_done_S = {}
    t_T = {}
    stt_done_V = {}
    sgrp_done_T = {}
    seen_S = set()
    seen_AV = set()
    phase_S_end = {}
    phase_AV_end = {}

    def emit_S(g, p, i):
        par = g % 2
        q4 = g % 4
        diag_a = (i == 2 * p)
        diag_b = (i == 2 * p + 1)
        diag = diag_a or diag_b
        if g >= 2:
            wt("t", "s", evac_done_S.get(g - 2, 0))
        else:
            wt("t", "s", pre_end_S)
        if p not in seen_S:
            seen_S.add(p)
            GG = p // 2
            wt("t", "kl%d" % (GG % 2), 64 * (GG // 2 + 1))
        if p == 0:
            wt("t", "s", q_done_qq[i // 4])
        G, b = p // 2, p % 2

        def k_rhs(c, h):
            if G % 2 == 0:
                buf = KPB_lo if h == 0 else KPB_hi
                return buf[:, :].rearrange("p (rr cc bb q) -> p rr cc bb q",
                                           rr=2, cc=8, bb=2)[:, :, c, b]
            return KPA[:, :].rearrange("p (r cc bb q) -> p r cc bb q",
                                       r=4, cc=8, bb=2)[:, 2 * h:2 * h + 2, c, b]

        t_last = None
        for c in range(7):
            for h in range(2):
                top(lambda e, c=c, h=h, par=par, i=i, rhs=k_rhs(c, h): e.matmul(
                    PS_S[par][:, h * 512:(h + 1) * 512],
                    QF[:, c * 2048 + i * P: c * 2048 + (i + 1) * P],
                    rhs, start=(c == 0), stop=False))
        if diag_a:
            for h in range(2):
                top(lambda e, h=h, par=par: e.matmul(
                    PS_S[par][:, h * 512:(h + 1) * 512], IDENT[:, :],
                    MASKA[:, h * 512:(h + 1) * 512], start=False, stop=False))
        elif diag_b:
            for r in range(4):
                top(lambda e, r=r, par=par: e.matmul(
                    PS_S[par][:, r * 256 + 128: r * 256 + 256], IDENT[:, :],
                    MASKA[:, r * 256: r * 256 + 128], start=False, stop=False))
        for h in range(2):
            t_last = top(lambda e, h=h, par=par, i=i, rhs=k_rhs(7, h): e.matmul(
                PS_S[par][:, h * 512:(h + 1) * 512],
                QF[:, 7 * 2048 + i * P: 7 * 2048 + (i + 1) * P],
                rhs, start=False, stop=True), inc=(h == 1))
        sgrp_done_T[g] = t_last
        if i == 15:
            phase_S_end[p] = cT.v
        wt("v", "t", t_last)
        v1 = vop(lambda e, par=par, q4=q4: e.tensor_reduce(
            NMAX[:, q4:q4 + 1], PS_S[par][:, :], AX.X, ALU.max, negate=True))
        wt("v", "v", v1)
        v2 = vop(lambda e, q4=q4, i=i: e.tensor_scalar(
            NMS[:, q4:q4 + 1], NMAX[:, q4:q4 + 1], 1.0 / 32.0,
            NMRUN[:, i:i + 1], ALU.mult, ALU.min))
        wt("v", "v", v2)
        v3 = vop(lambda e, q4=q4, i=i: e.tensor_tensor(
            DSC[:, q4:q4 + 1], NMS[:, q4:q4 + 1], NMRUN[:, i:i + 1], ALU.subtract))
        wt("v", "v", v3)
        vop(lambda e, q4=q4, i=i: e.tensor_copy(NMRUN[:, i:i + 1], NMS[:, q4:q4 + 1]))
        v3_done_V[g] = v3

    def emit_T(g, p, i):
        par = g % 2
        q4 = g % 4
        wt("s", "v", v3_done_V[g])
        wt("s", "t", t_T.get(g - 1, 0))
        s2 = sop(lambda e, par=par, q4=q4: e.activation(
            EB[:, :], PS_S[par][:, :], AF.Exp, scale=1.0 / 32.0,
            bias=NMS[:, q4:q4 + 1], accum_out=ACC[:, q4:q4 + 1]))
        exp_done_S[g] = s2
        wt("t", "s", exp_done_S[g])
        tl = None
        for e_ in range(8):
            tl = top(lambda e, e_=e_, par=par: e.matmul(
                PS_S[par][:, e_ * P:(e_ + 1) * P], EB[:, e_ * P:(e_ + 1) * P],
                IDENT[:, :], start=True, stop=True), inc=(e_ == 7))
        t_T[g] = tl
        wt("s", "t", tl)
        evac_done_S[g] = sop(lambda e, par=par, g=g: e.copy(
            ETS[g % 2][:, :], PS_S[par][:, :]))

    def emit_AV(g, p, i):
        q4 = g % 4
        wt("s", "v", v3_done_V[g])
        s1 = sop(lambda e, q4=q4: e.activation(
            ALS[:, q4:q4 + 1], DSC[:, q4:q4 + 1], AF.Exp))
        wt("t", "s", evac_done_S[g])
        wt("t", "v", stt_done_V.get(g - 1, 0))
        if p not in seen_AV:
            seen_AV.add(p)
            wt("t", "vl%d" % (p % 2), 64 * (p // 2 + 1))
        VB = VBUF[p % 2]
        avb = None
        for e_ in range(8):
            for j in range(2):
                avb = top(lambda e, e_=e_, j=j, g=g, VB=VB: e.matmul(
                    PS_AV[:, j * 512:(j + 1) * 512],
                    ETS[g % 2][:, e_ * P:(e_ + 1) * P],
                    VB[:, e_ * D + j * 512: e_ * D + j * 512 + 512],
                    start=(e_ == 0), stop=(e_ == 7)),
                    inc=(e_ == 7 and j == 1))
        wt("v", "s", s1)
        wt("v", "s", exp_done_S[g])
        vop(lambda e, q4=q4, i=i: e.scalar_tensor_tensor(
            LRUN[:, i:i + 1], LRUN[:, i:i + 1], ALS[:, q4:q4 + 1],
            ACC[:, q4:q4 + 1], ALU.mult, ALU.add))
        wt("v", "t", avb)
        stt_done_V[g] = vop(lambda e, q4=q4, i=i: e.scalar_tensor_tensor(
            O[:, i * D:(i + 1) * D], O[:, i * D:(i + 1) * D], ALS[:, q4:q4 + 1],
            PS_AV[:, :], ALU.mult, ALU.add))
        if i == 15:
            phase_AV_end[p] = cT.v
            if p in (1, 3):
                wt("y", "t", phase_S_end[p])
                land_K_pair(p // 2 + 2)
            if p <= 5:
                wt("g", "t", phase_AV_end[p])
                land_V(p + 2)

    g = 0
    tq = []
    aq = []
    for p in range(NBLK):
        for i in range(2 * p, NTILE):
            emit_S(g, p, i)
            tq.append((g, p, i))
            if len(tq) >= 2:
                u = tq.pop(0)
                emit_T(*u)
                aq.append(u)
            if len(aq) >= 2:
                emit_AV(*aq.pop(0))
            g += 1
    while tq or aq:
        if tq:
            u = tq.pop(0)
            emit_T(*u)
            aq.append(u)
        elif aq:
            emit_AV(*aq.pop(0))

    last_stt_V = stt_done_V[g - 1]
    last_sgrp_T = sgrp_done_T[g - 1]
    last_avb_T = cT.v
    last_tT = t_T[g - 1]

    wt("v", "v", cV.v)
    v_nmin = vop(lambda e: e.tensor_reduce(NMIN[:, :], NMRUN[:, :], AX.X, ALU.min))
    wt("s", "v", v_nmin)
    s_nb = sop(lambda e: e.copy(NMINB[:, :], NMIN[:, :]))
    wt("s", "t", last_sgrp_T)
    s_srow = pr_small(lambda e, prb: e.matmul(prb[0:1, 0:P], NMINB[:, :],
                                              IDENT[:, :], start=True, stop=True),
                      lambda e, prb: e.copy(SROW[:, :], prb[0:1, 0:P]), ws=s_nb)
    wt("v", "s", s_srow)
    v_nmg = vop(lambda e: e.tensor_reduce(MG[:, :], SROW[:, :], AX.X, ALU.min))
    s_b1 = pr_small(lambda e, prb: e.matmul(prb[:, 0:1], ONESR[:, :], MG[:, :],
                                            start=True, stop=True),
                    lambda e, prb: e.copy(DLB[:, :], prb[:, 0:1]), wv=v_nmg)
    wt("s", "s", s_b1)
    s_exs = sop(lambda e: e.activation(EXS[:, :], NMRUN[:, :], AF.Exp, scale=-1.0,
                                       bias=DLB[:, 0:1]))
    wt("v", "s", s_exs)
    v_z = vop(lambda e: e.tensor_tensor(EXS[:, :], EXS[:, :], LRUN[:, :], ALU.mult))
    wt("v", "v", v_z)
    v_zc = vop(lambda e: e.tensor_reduce(ZC[:, :], EXS[:, :], AX.X, ALU.add))
    s_sc2 = pr_small(lambda e, prb: e.matmul(prb[0:1, 0:1], ZC[:, :], ONESF[:, :],
                                             start=True, stop=True),
                     lambda e, prb: e.copy(SCP[0:1, 1:2], prb[0:1, 0:1]), wv=v_zc)
    v_mc = vop(lambda e: e.tensor_scalar(SCP[0:1, 0:1], MG[0:1, 0:1], -1.0, None,
                                         ALU.mult))
    wt("g", "v", v_mc)
    wt("g", "s", s_sc2)
    g_pack = gdma(lambda e: e.dma_start(out=cc_in.ap(), in_=SCP[:, :]))
    gbar()

    cc_stats = emit_cc(cc_in, cc_out, 0)

    wst_free_V = [0, 0]
    for j in range(8):
        par = j % 2
        wt("y", "t", last_avb_T)
        wt("y", "v", wst_free_V[par])
        g_wj = ydma(lambda e, j=j, par=par: e.dma_start(
            out=WST[par][:, :], in_=wt_d[j * P:(j + 1) * P, :]), "x%d" % par)
        wt("v", "x%d" % par, g_wj)
        wst_free_V[par] = vop(lambda e, j=j, par=par: e.tensor_reduce(
            ABSA[:, j:j + 1], WST[par][:, :], AX.X, ALU.add,
            apply_absolute_value=True))
    wt("v", "v", cV.v)
    v_abs2 = vop(lambda e: e.tensor_reduce(NMIN[:, :], ABSA[:, :], AX.X, ALU.add))
    s_dlt = pr_small(lambda e, prb: e.matmul(prb[0:1, 0:1], NMIN[:, :], ONESF[:, :],
                                             start=True, stop=True),
                     lambda e, prb: e.activation(DLT[:, :], prb[0:1, 0:1], AF.Identity,
                                                 scale=0.7 / (1024.0 * 1024.0)),
                     wv=v_abs2)
    s_db = pr_small(lambda e, prb: e.matmul(prb[:, 0:1], ONESR[:, :], DLT[:, :],
                                            start=True, stop=True),
                    lambda e, prb: e.copy(DLB[:, :], prb[:, 0:1]), ws=s_dlt)
    wt("v", "s", s_db)
    v_ndb = vop(lambda e: e.tensor_scalar(NDLB[:, :], DLB[:, :], -1.0, None, ALU.mult))
    stil_free_T = [0]
    v_st2 = 0
    for j in range(8):
        par = j % 2
        wt("y", "v", wst_free_V[par])
        g_wj = ydma(lambda e, j=j, par=par: e.dma_start(
            out=WST[par][:, :], in_=wt_d[j * P:(j + 1) * P, :]), "x%d" % par)
        wt("v", "x%d" % par, g_wj)
        wt("v", "v", v_ndb)
        if j > 0:
            wt("v", "v", v_st2)
        vop(lambda e, par=par: e.tensor_scalar(
            GTA[:, :], WST[par][:, :], DLB[:, 0:1], None, ALU.is_gt))
        v_cmp = vop(lambda e, par=par: e.tensor_scalar(
            GTB[:, :], WST[par][:, :], NDLB[:, 0:1], None, ALU.is_lt))
        wst_free_V[par] = v_cmp
        wt("v", "v", v_cmp)
        v_sub = vop(lambda e: e.tensor_tensor(GTA[:, :], GTA[:, :], GTB[:, :],
                                              ALU.subtract))
        wt("v", "v", v_sub)
        wt("v", "t", stil_free_T[0])
        v_st2 = vop(lambda e: e.tensor_scalar(STIL[:, :], GTA[:, :], 2.0, None,
                                              ALU.mult))
        for half in range(2):
            slot = pr_n[0] % 2
            pr_n[0] += 1
            prb = PRB[slot]
            wt("t", "v", v_st2)
            wt("t", "s", pr_done[slot])
            t_g2 = None
            for cc in range(4):
                c = half * 4 + cc
                t_g2 = top(lambda e, c=c, cc=cc, prb=prb: e.matmul(
                    prb[:, cc * P:(cc + 1) * P], STIL[:, c * P:(c + 1) * P],
                    IDENT[:, :], start=True, stop=True), inc=(cc == 3))
            wt("s", "t", t_g2)
            for cc in range(4):
                c = half * 4 + cc
                if cc == 3:
                    pr_done[slot] = sop(lambda e, c=c, cc=cc, j=j, prb=prb: e.copy(
                        WTB[:, c * D + j * P: c * D + (j + 1) * P],
                        prb[:, cc * P:(cc + 1) * P]))
                else:
                    sop(lambda e, c=c, cc=cc, j=j, prb=prb: e.copy(
                        WTB[:, c * D + j * P: c * D + (j + 1) * P],
                        prb[:, cc * P:(cc + 1) * P]))
        stil_free_T[0] = cT.v
    s_wtb = pr_done[(pr_n[0] - 1) % 2]
    g_al = gdma(lambda e: e.dma_start(out=ALV[:, :], in_=al_in))
    wt("g", "t", last_avb_T)
    g_fb = gdma(lambda e: e.dma_start(out=FBROW[:, :], in_=fb_in))
    gbar()
    wt("v", "g", g_fb)
    v_ia = vop(lambda e: e.reciprocal(T1[:, :], ALV[:, :]))
    wt("v", "v", v_ia)
    v_i2 = vop(lambda e: e.tensor_scalar(T1[:, :], T1[:, :], 2.0, None, ALU.mult))
    wt("v", "v", v_i2)
    wt("v", "t", last_tT)
    v_bs = vop(lambda e: e.tensor_scalar(BSROW[:, :], FBROW[:, :], T1[0:1, 0:1],
                                         None, ALU.mult))
    v_ah = vop(lambda e: e.tensor_scalar(PAIR[0:1, 0:1], ALV[0:1, 0:1], 0.5, None,
                                         ALU.mult))
    s_ahb = pr_small(lambda e, prb: e.matmul(prb[:, 0:1], ONESR[:, :], PAIR[0:1, 0:1],
                                             start=True, stop=True),
                     lambda e, prb: e.copy(AHB[:, :], prb[:, 0:1]), wv=v_ah)

    wt("g", cc_stats, 1)
    g_g8 = gdma(lambda e: e.dma_start(
        out=G8[:, :], in_=cc_out.ap().rearrange("(a r) t -> a (r t)", a=1)))
    gbar()
    wt("v", "g", g_g8)
    v_mg = vop(lambda e: e.tensor_reduce(MG[:, :], G8[:, 0:8:2], AX.X, ALU.max))
    wt("v", "v", v_mg)
    v_ng2 = vop(lambda e: e.tensor_scalar(NMG2[:, :], MG[:, :], -1.0, None, ALU.mult))
    wt("s", "v", v_ng2)
    s_er = sop(lambda e: e.activation(ER4[:, :], G8[:, 0:8:2], AF.Exp,
                                      bias=NMG2[0:1, 0:1]))
    wt("v", "s", s_er)
    v_zr = vop(lambda e: e.tensor_tensor(ZR4[:, :], ER4[:, :], G8[:, 1:8:2], ALU.mult))
    wt("v", "v", v_zr)
    v_zs = vop(lambda e: e.tensor_reduce(ZS[:, :], ZR4[:, :], AX.X, ALU.add))
    wt("s", "v", v_zs)
    s_lnz = sop(lambda e: e.activation(IVZ[:, :], ZS[:, :], AF.Ln))
    wt("v", "s", s_lnz)
    v_pair = vop(lambda e: e.tensor_tensor(PAIR[0:1, 0:1], NMG2[0:1, 0:1],
                                           IVZ[0:1, 0:1], ALU.subtract))
    s_fb2 = pr_small(lambda e, prb: e.matmul(prb[:, 0:1], ONESR[:, :], PAIR[0:1, 0:1],
                                             start=True, stop=True),
                     lambda e, prb: e.copy(FB2[:, 0:1], prb[:, 0:1]), wv=v_pair)

    ost_free_ST = [0, 0]
    x2_done_V = {}
    FSLOT = [PS_AV, PS_S[0], PS_S[1]]
    fs_done = [("v", last_stt_V),
               ("s", evac_done_S.get(g - 2, 0)),
               ("s", evac_done_S.get(g - 1, 0))]
    apply_S = {}
    mv_guard = [apply_guard_S[0], apply_guard_S[1]]
    rstd_guard = [0, 0]
    f2_S = {}
    aggr_V = {}
    ffn_T = {}
    lnfm_S = {}

    def ffn_a(i):
        par = i % 2
        wt("y", "v", x2_done_V.get(i - 2, 0))
        if i < 2:
            wt("y", "st1", yVout[3])
            wt("y", "v", wst_free_V[par])
        xw = ydma(lambda e, i=i, par=par: e.dma_start(
            out=XB[par][:, :], in_=xq[i * P:(i + 1) * P, :]), "x%d" % par)
        wt("s", "s", s_fb2)
        wt("s", "v", x2_done_V.get(i - 2, 0))
        s_f = sop(lambda e, i=i, par=par: e.activation(
            F2[:, par:par + 1], NMRUN[:, i:i + 1], AF.Exp, scale=-1.0,
            bias=FB2[:, 0:1]))
        f2_S[i] = s_f
        wt("v", "s", s_f)
        wt("v", "x%d" % par, xw)
        v_x2 = vop(lambda e, i=i, par=par: e.scalar_tensor_tensor(
            O[:, i * D:(i + 1) * D], O[:, i * D:(i + 1) * D], F2[:, par:par + 1],
            XB[par][:, :], ALU.mult, ALU.add))
        x2_done_V[i] = v_x2
        wt("v", "s", mv_guard[par])
        wt("v", "v", v_x2)
        vop(lambda e, i=i, par=par: e.bn_stats(
            BN6[:, par * 12:par * 12 + 6], O[:, i * D:i * D + 512]))
        v_bn = vop(lambda e, i=i, par=par: e.bn_stats(
            BN6[:, par * 12 + 6:par * 12 + 12], O[:, i * D + 512:(i + 1) * D]))
        wt("v", "v", v_bn)
        aggr_V[i] = vop(lambda e, par=par: e.bn_aggr(
            MV[:, par * 2:par * 2 + 2], BN6[:, par * 12:par * 12 + 12]))

    def ffn_b(i):
        par = i % 2
        wt("s", "v", aggr_V[i])
        s_ln = sop(lambda e, par=par: e.activation(
            LNV[:, par:par + 1], MV[:, par * 2 + 1:par * 2 + 2], AF.Ln,
            bias=EPSC[:, 0:1]))
        mv_guard[par] = s_ln
        wt("s", "s", s_ln)
        wt("s", "v", rstd_guard[par])
        s_rs = sop(lambda e, par=par: e.activation(
            RSTD[:, par:par + 1], LNV[:, par:par + 1], AF.Exp, scale=-0.5))
        wt("v", "s", s_rs)
        wt("v", "t", lnt_free_T[par])
        v_ap = vop(lambda e, i=i, par=par: e.tensor_scalar(
            LNT[par][:, :], O[:, i * D:(i + 1) * D], MV[:, par * 2:par * 2 + 1],
            RSTD[:, par:par + 1], ALU.subtract, ALU.mult))
        rstd_guard[par] = v_ap
        apply_S[i] = v_ap

    def ffn_c(i):
        par = i % 2
        slot = i % 3
        for half in range(2):
            pslot = pr_n[0] % 2
            pr_n[0] += 1
            prb = PRB[pslot]
            wt("t", "v", apply_S[i])
            wt("t", "s", pr_done[pslot])
            t_grp = None
            for cc in range(4):
                c = half * 4 + cc
                t_grp = top(lambda e, c=c, cc=cc, prb=prb, par=par: e.matmul(
                    prb[:, cc * P:(cc + 1) * P],
                    LNT[par][:, c * P:(c + 1) * P], IDENT[:, :],
                    start=True, stop=True), inc=(cc == 3))
            wt("s", "t", t_grp)
            for cc in range(4):
                c = half * 4 + cc
                if cc == 3:
                    pr_done[pslot] = sop(lambda e, c=c, cc=cc, prb=prb: e.copy(
                        LN2F[:, c * P:(c + 1) * P], prb[:, cc * P:(cc + 1) * P]))
                else:
                    sop(lambda e, c=c, cc=cc, prb=prb: e.copy(
                        LN2F[:, c * P:(c + 1) * P], prb[:, cc * P:(cc + 1) * P]))
        lnt_free_T[par] = cT.v
        lnfm_S[i] = pr_done[(pr_n[0] - 1) % 2]
        wt("t", "s", lnfm_S[i])
        if i == 0:
            wt("t", "s", s_wtb)
            wt("t", "v", v_bs)
        wt("t", fs_done[slot][0], fs_done[slot][1])
        t_ffn = None
        for c in range(8):
            for j in range(2):
                top(lambda e, c=c, j=j, slot=slot: e.matmul(
                    FSLOT[slot][:, j * 512:(j + 1) * 512], LN2F[:, c * P:(c + 1) * P],
                    WTB[:, c * D + j * 512: c * D + j * 512 + 512],
                    start=(c == 0), stop=False))
        for j in range(2):
            t_ffn = top(lambda e, j=j, slot=slot: e.matmul(
                FSLOT[slot][:, j * 512:(j + 1) * 512], ONESC[:, :],
                BSROW[0:1, j * 512:(j + 1) * 512], start=False, stop=True),
                inc=(j == 1))
        ffn_T[i] = t_ffn

    def ffn_d(i):
        par = i % 2
        slot = i % 3
        wt("v", "t", ffn_T[i])
        wt("v", "s", s_ahb)
        wt("v", "st%d" % par, ost_free_ST[par])
        ep = vop(lambda e, i=i, par=par, slot=slot: e.scalar_tensor_tensor(
            OST[par][:, :], FSLOT[slot][:, :], AHB[:, 0:1], O[:, i * D:(i + 1) * D],
            ALU.mult, ALU.add))
        fs_done[slot] = ("v", ep)
        wt("y", "v", ep)
        st_val = ydma(lambda e, i=i, par=par: e.dma_start(
            out=y[i * P:(i + 1) * P, :], in_=OST[par][:, :]), "st%d" % par)
        ost_free_ST[par] = st_val

    for i in range(NTILE + 3):
        if i < NTILE:
            ffn_a(i)
        if 1 <= i < NTILE + 1:
            ffn_b(i - 1)
        if 2 <= i < NTILE + 2:
            ffn_c(i - 2)
        if i >= 3:
            ffn_d(i - 3)

    wt("y", "st0", cST[0].v)
    wt("y", "st1", cST[1].v)

    with (
        nc.semaphore("sem_t") as sem_t,
        nc.semaphore("sem_s") as sem_s,
        nc.semaphore("sem_v") as sem_v,
        nc.semaphore("sem_x0") as sem_x0,
        nc.semaphore("sem_x1") as sem_x1,
        nc.semaphore("sem_x2") as sem_x2,
        nc.semaphore("sem_x3") as sem_x3,
        nc.semaphore("sem_g") as sem_g,
        nc.semaphore("sem_st0") as sem_st0,
        nc.semaphore("sem_st1") as sem_st1,
        nc.semaphore("sem_kl0") as sem_kl0,
        nc.semaphore("sem_kl1") as sem_kl1,
        nc.semaphore("sem_vl0") as sem_vl0,
        nc.semaphore("sem_vl1") as sem_vl1,
        nc.Block() as block,
    ):
        S = {"t": sem_t, "s": sem_s, "v": sem_v, "x0": sem_x0, "x1": sem_x1,
             "x2": sem_x2, "x3": sem_x3,
             "g": sem_g, "st0": sem_st0, "st1": sem_st1,
             "kl0": sem_kl0, "kl1": sem_kl1, "vl0": sem_vl0, "vl1": sem_vl1}
        for ci in range(cc_n[0]):
            S["cc%d" % ci] = nc.alloc_semaphore(name="sem_cc%d" % ci)

        @block.tensor
        def _(t):
            for f in ops["t"]:
                f(t, S)

        @block.scalar
        def _(s):
            for f in ops["s"]:
                f(s, S)

        @block.vector
        def _(v):
            for f in ops["v"]:
                f(v, S)

        @block.gpsimd
        def _(gp):
            for f in ops["g"]:
                f(gp, S)

        @block.sync
        def _(sy):
            for f in ops["y"]:
                f(sy, S)

    return nc


_NC_CACHE = None


def _get_nc():
    global _NC_CACHE
    if _NC_CACHE is None:
        _NC_CACHE = build()
    return _NC_CACHE


def kernel(L0_attn_x, L0_attn_Wq, L0_attn_bq, L0_attn_Wk, L0_attn_bk,
           L0_attn_Wv, L0_attn_bv, L0_ffn_weight, L0_ffn_alpha, L0_ffn_bias):
    x = np.ascontiguousarray(np.asarray(L0_attn_x, dtype=np.float32))
    wq_h = np.ascontiguousarray(np.asarray(L0_attn_Wq, dtype=np.float32))
    wk_h = np.ascontiguousarray(np.asarray(L0_attn_Wk, dtype=np.float32))
    wv_h = np.ascontiguousarray(np.asarray(L0_attn_Wv, dtype=np.float32))
    wt_h = np.ascontiguousarray(np.asarray(L0_ffn_weight, dtype=np.float32))
    bq_h = np.ascontiguousarray(np.asarray(L0_attn_bq, dtype=np.float32).reshape(8, P).T)
    bk_h = np.ascontiguousarray(np.asarray(L0_attn_bk, dtype=np.float32).reshape(8, P).T)
    bv_h = np.ascontiguousarray(np.asarray(L0_attn_bv, dtype=np.float32).reshape(1, D))
    fb_h = np.ascontiguousarray(np.asarray(L0_ffn_bias, dtype=np.float32).reshape(1, D))
    al_h = np.ascontiguousarray(np.asarray(L0_ffn_alpha, dtype=np.float32).reshape(1, 1))
    ident = np.eye(P, dtype=np.float32).astype(ml_dtypes.bfloat16)
    onescol = np.ones((1, P), dtype=np.float32).astype(ml_dtypes.bfloat16)

    r_ = np.arange(P)[:, None]
    colv = np.arange(D)[None, :]
    rk = colv // 256
    hp = (colv // 128) % 2
    tt = colv % 128
    mt = 4 * hp + rk

    in_maps = []
    for core in range(8):
        b = core // 4
        c = core % 4
        xb_full = x[b]
        xq_h = np.ascontiguousarray(xb_full.reshape(64, P, D)[c::4].reshape(2048, D))
        allow_a = (mt < c) | ((mt == c) & (tt <= r_))
        maska = np.where(allow_a, 0.0, -1e30).astype(np.float32)
        in_maps.append({
            "xq": xq_h,
            "wq": wq_h, "wk": wk_h, "wv": wv_h, "wt": wt_h,
            "bq": bq_h, "bk": bk_h, "bv": bv_h, "fb": fb_h, "alpha": al_h,
            "maska": maska.astype(ml_dtypes.bfloat16),
            "ident": ident, "onescol": onescol,
        })

    nc = _get_nc()
    res = run_bass_kernel_spmd(nc, in_maps, core_ids=list(range(8)))

    out = np.empty((2, T, D), dtype=np.float32)
    for core in range(8):
        b = core // 4
        c = core % 4
        yc = np.asarray(res.results[core]["y"], dtype=np.float32)
        out[b].reshape(64, P, D)[c::4] = yc.reshape(NTILE, P, D)
    return out


# version 19
# speedup vs baseline: 1.0446x; 1.0446x over previous
import numpy as np
import ml_dtypes
import concourse.bass as bass
import concourse.mybir as mybir
from concourse.bass_utils import run_bass_kernel_spmd

F32 = mybir.dt.float32
BF16 = mybir.dt.bfloat16
AF = mybir.ActivationFunctionType
ALU = mybir.AluOpType
AX = mybir.AxisListType

P = 128
D = 1024
T = 8192
NTILE = 16
NBLK = 8
NEG_BIG = 3.0e38
EPS = 1e-5


def build():
    nc = bass.Bass("TRN2", target_bir_lowering=False)

    xq = nc.dram_tensor("xq", [2048, D], F32, kind="ExternalInput").ap()
    wq_d = nc.dram_tensor("wq", [D, D], F32, kind="ExternalInput").ap()
    wk_d = nc.dram_tensor("wk", [D, D], F32, kind="ExternalInput").ap()
    wv_d = nc.dram_tensor("wv", [D, D], F32, kind="ExternalInput").ap()
    wt_d = nc.dram_tensor("wt", [D, D], F32, kind="ExternalInput").ap()
    bq_in = nc.dram_tensor("bq", [P, 8], F32, kind="ExternalInput").ap()
    bk_in = nc.dram_tensor("bk", [P, 8], F32, kind="ExternalInput").ap()
    bv_in = nc.dram_tensor("bv", [1, D], F32, kind="ExternalInput").ap()
    fb_in = nc.dram_tensor("fb", [1, D], F32, kind="ExternalInput").ap()
    al_in = nc.dram_tensor("alpha", [1, 1], F32, kind="ExternalInput").ap()
    maska_in = nc.dram_tensor("maska", [P, D], BF16, kind="ExternalInput").ap()
    ident_in = nc.dram_tensor("ident", [P, P], BF16, kind="ExternalInput").ap()
    onescol_in = nc.dram_tensor("onescol", [1, P], BF16, kind="ExternalInput").ap()

    y = nc.dram_tensor("y", [2048, D], F32, kind="ExternalOutput").ap()

    cc_in = nc.dram_tensor("cc_in", [1, 2], F32)
    cc_out = nc.dram_tensor("cc_out", [4, 2], F32)
    ccKV_in = [nc.dram_tensor("cckv_in%d" % i, [2048, 512], BF16) for i in range(4)]
    ccKV_out = [nc.dram_tensor("cckv_out%d" % i, [8192, 512], BF16) for i in range(4)]
    ccK0_in = nc.dram_tensor("cck0_in", [1024, 512], BF16)
    ccK0_out = nc.dram_tensor("cck0_out", [4096, 512], BF16)
    ccV0_in = nc.dram_tensor("ccv0_in", [1024, 512], BF16)
    ccV0_out = nc.dram_tensor("ccv0_out", [4096, 512], BF16)

    SBUF_BASE = 16512
    nc.alloc_sbuf_tensor("_spacer", [P, (SBUF_BASE + 212608 - 16488) // 4], F32)

    def sbat(name, shape, dtype, off):
        return nc.alloc_sbuf_tensor_at(name, shape, dtype, offset=SBUF_BASE + off)

    O = sbat("O", [P, 16 * D], F32, 0)
    QF = sbat("QF", [P, 8 * 2048], BF16, 65536)
    WKB = sbat("WKB", [P, 8 * D], BF16, 98304)
    KPB_lo = sbat("KPB_lo", [P, 8 * D], BF16, 98304)
    WVB = sbat("WVB", [P, 8 * D], BF16, 114688)
    VB1 = sbat("VB1", [P, 8 * D], BF16, 114688)
    WQB = sbat("WQB", [P, 8 * D], BF16, 131072)
    KPB_hi = sbat("KPB_hi", [P, 8 * D], BF16, 131072)
    WTB = sbat("WTB", [P, 8 * D], BF16, 131072)
    KPA = sbat("KPA", [P, 16 * D], BF16, 163840)
    FST = sbat("FST", [P, 8 * D], BF16, 147456)
    VB0 = sbat("VB0", [P, 8 * D], BF16, 147456)
    LNQF = sbat("LNQF", [P, 8 * D], BF16, 163840)
    XB = [sbat("XB0", [P, D], F32, 188416),
          sbat("XB1", [P, D], F32, 192512),
          sbat("XB2", [P, D], F32, 180224),
          sbat("XB3", [P, D], F32, 184320)]
    VSTG = sbat("VSTG", [P, 4 * D], BF16, 188416)
    LNT = [sbat("LNT0", [P, D], BF16, 196608),
           sbat("LNT1", [P, D], BF16, 202752)]
    ETS1 = sbat("ETS1", [P, D], BF16, 198656)
    MASKA = sbat("MASKA", [P, D], BF16, 200704)
    EB = sbat("EB", [P, D], BF16, 204800)
    ETS0 = sbat("ETS0", [P, D], BF16, 206848)
    ETS = [ETS0, ETS1]
    WST = [sbat("WST0", [P, D], F32, 147456),
           sbat("WST1", [P, D], F32, 151552)]
    OST = [sbat("OST0", [P, D], F32, 155648),
           sbat("OST1", [P, D], F32, 159744)]
    GTA = sbat("GTA", [P, D], F32, 163840)
    GTB = sbat("GTB", [P, D], F32, 167936)
    STIL = sbat("STIL", [P, D], BF16, 172032)
    FBROW = sbat("FBROW", [1, D], BF16, 206848)
    BSROW = sbat("BSROW", [1, D], BF16, 204800)
    LN2F = sbat("LN2F", [P, 8 * P], BF16, 180224)
    SROW = sbat("SROW", [1, P], F32, 200704)
    ABSA = sbat("ABSA", [P, 8], F32, 201216)
    DLT = sbat("DLT", [1, 1], F32, 201248)
    DLB = sbat("DLB", [P, 1], F32, 201280)
    NDLB = sbat("NDLB", [P, 1], F32, 201312)
    ALV = sbat("ALV", [1, 1], F32, 201344)
    T1 = sbat("T1", [1, 1], F32, 201376)
    SCP = sbat("SCP", [1, 2], F32, 201408)
    G8 = sbat("G8", [1, 8], F32, 201440)
    MG = sbat("MG", [1, 1], F32, 201472)
    NMG2 = sbat("NMG2", [1, 1], F32, 201504)
    ER4 = sbat("ER4", [1, 4], F32, 201536)
    ZR4 = sbat("ZR4", [1, 4], F32, 201568)
    ZS = sbat("ZS", [1, 1], F32, 201600)
    IVZ = sbat("IVZ", [1, 1], F32, 201632)
    PAIR = sbat("PAIR", [1, 2], F32, 201664)
    FB2 = sbat("FB2", [P, 2], F32, 201696)
    NMINB = sbat("NMINB", [P, 1], BF16, 201728)
    F2 = sbat("F2", [P, 2], F32, 201760)
    AHB = sbat("AHB", [P, 1], F32, 201792)
    NMIN = sbat("NMIN", [P, 1], F32, 201824)
    ZC = sbat("ZC", [P, 1], F32, 201856)
    EXS = sbat("EXS", [P, NTILE], F32, 201888)
    BVROW = sbat("BVROW", [1, D], BF16, 208896)
    ONESR = sbat("ONESR", [1, P], F32, 210944)
    IDENT = sbat("IDENT", [P, P], BF16, 211456)
    ONESC = sbat("ONESC", [1, P], BF16, 211712)
    BQ = sbat("BQ", [P, 8], F32, 211968)
    BK = sbat("BK", [P, 8], F32, 212000)
    ONESF = sbat("ONESF", [P, 1], F32, 212032)
    EPSC = sbat("EPSC", [P, 1], F32, 212064)
    NMRUN = sbat("NMRUN", [P, NTILE], F32, 212096)
    LRUN = sbat("LRUN", [P, NTILE], F32, 212160)
    BN6 = sbat("BN6", [P, 24], F32, 212224)
    MV = sbat("MV", [P, 4], F32, 212320)
    LNV = sbat("LNV", [P, 2], F32, 212352)
    RSTD = sbat("RSTD", [P, 2], F32, 212384)
    NMR = sbat("NMR", [P, 2], F32, 212416)
    NMAX = sbat("NMAX", [P, 4], F32, 212448)
    NMS = sbat("NMS", [P, 4], F32, 212480)
    DSC = sbat("DSC", [P, 4], F32, 212512)
    ALS = sbat("ALS", [P, 4], F32, 212544)
    ACC = sbat("ACC", [P, 4], F32, 212576)

    VBUF = [VB0, VB1]

    PS_S = [nc.alloc_psum_tensor("ps_sa", [P, D], F32),
            nc.alloc_psum_tensor("ps_sb", [P, D], F32)]
    PS_AV = nc.alloc_psum_tensor("ps_av", [P, D], F32)
    PS_PR = nc.alloc_psum_tensor("ps_pr", [P, 512], F32)
    PS_PR2 = nc.alloc_psum_tensor("ps_pr2", [P, 512], F32)

    ops = {"t": [], "v": [], "s": [], "g": [], "y": []}

    class Ctr:
        def __init__(self):
            self.v = 0

        def inc(self, n=1):
            self.v += n
            return self.v

    cT, cS, cV, cG = Ctr(), Ctr(), Ctr(), Ctr()
    cX = [Ctr(), Ctr(), Ctr(), Ctr()]
    cST = [Ctr(), Ctr()]
    cKL = [Ctr(), Ctr()]
    cVL = [Ctr(), Ctr()]
    YCTR = {"x0": cX[0], "x1": cX[1], "x2": cX[2], "x3": cX[3],
            "st0": cST[0], "st1": cST[1],
            "kl0": cKL[0], "kl1": cKL[1], "vl0": cVL[0], "vl1": cVL[1]}

    def wt(eng, sem, val):
        if val and val > 0:
            ops[eng].append(lambda e, S: e.wait_ge(S[sem], val))

    def top(fn, inc=False):
        if inc:
            def f(e, S):
                fn(e).then_inc(S["t"], 1)
            ops["t"].append(f)
            return cT.inc()
        ops["t"].append(lambda e, S: fn(e))
        return None

    def vop(fn):
        def f(e, S):
            fn(e).then_inc(S["v"], 1)
        ops["v"].append(f)
        return cV.inc()

    def sop(fn):
        def f(e, S):
            fn(e).then_inc(S["s"], 1)
        ops["s"].append(f)
        return cS.inc()

    def gdma(fn):
        def f(e, S):
            fn(e).then_inc(S["g"], 16)
        ops["g"].append(f)
        return cG.inc(16)

    def ydma(fn, sem):
        def f(e, S):
            fn(e).then_inc(S[sem], 16)
        ops["y"].append(f)
        return YCTR[sem].inc(16)

    def gbar():
        val = cG.v
        ops["g"].append(lambda e, S: e.wait_ge(S["g"], val))

    gdma(lambda e: e.dma_start(out=IDENT[:, :], in_=ident_in))
    gdma(lambda e: e.dma_start(out=MASKA[:, :], in_=maska_in))
    gdma(lambda e: e.dma_start(out=ONESC[:, :], in_=onescol_in))
    gdma(lambda e: e.dma_start(out=BQ[:, :], in_=bq_in))
    gdma(lambda e: e.dma_start(out=BK[:, :], in_=bk_in))
    g_smalls = gdma(lambda e: e.dma_start(out=BVROW[:, :], in_=bv_in))

    def wload(dst, src):
        return gdma(lambda e: e.dma_start(
            out=dst[:, :].rearrange("p (e mc) -> p e mc", e=8),
            in_=src.rearrange("(e p) mc -> p e mc", p=P)))

    gbar()
    g_wq = wload(WQB, wq_d)
    g_wk = wload(WKB, wk_d)
    g_wv = wload(WVB, wv_d)
    g_aux_all = g_wv
    gbar()

    vop(lambda e: e.memset(ONESF[:, :], 1.0))
    vop(lambda e: e.memset(ONESR[:, :], 1.0))
    vop(lambda e: e.memset(EPSC[:, :], EPS))
    vop(lambda e: e.memset(NMRUN[:, :], NEG_BIG))
    vop(lambda e: e.memset(LRUN[:, :], 0.0))
    for j in range(4):
        vop(lambda e, j=j: e.memset(O[:, j * 4096:(j + 1) * 4096], 0.0))

    lnt_free_T = [0, 0]
    apply_guard_S = [0, 0]
    PRB = [PS_PR, PS_PR2]
    pr_n = [0]
    pr_done = [0, 0]
    DSLOT = [(PS_PR, PS_PR2),
             (PS_S[0][:, 0:512], PS_S[0][:, 512:1024]),
             (PS_S[1][:, 0:512], PS_S[1][:, 512:1024])]
    ds_n = [0]
    ds_done = [0, 0, 0]
    xb_free = [0, 0, 0, 0]

    def xload(src_rows, xpar):
        wt("y", "s", xb_free[xpar])
        return ydma(lambda e: e.dma_start(out=XB[xpar][:, :], in_=src_rows),
                    "x%d" % xpar)

    def ln_task(par, x_wait, xpar=None):
        if xpar is None:
            xpar = par
        wt("v", "s", apply_guard_S[par])
        wt("v", "x%d" % xpar, cX[xpar].v)
        vop(lambda e: e.bn_stats(BN6[:, par * 12:par * 12 + 6], XB[xpar][:, 0:512]))
        v_bn = vop(lambda e: e.bn_stats(BN6[:, par * 12 + 6:par * 12 + 12],
                                        XB[xpar][:, 512:1024]))
        wt("v", "v", v_bn)
        v_ag = vop(lambda e: e.bn_aggr(MV[:, par * 2:par * 2 + 2],
                                       BN6[:, par * 12:par * 12 + 12]))
        wt("s", "v", v_ag)
        s_ln = sop(lambda e: e.activation(LNV[:, par:par + 1], MV[:, par * 2 + 1:par * 2 + 2],
                                          AF.Ln, bias=EPSC[:, 0:1]))
        wt("s", "s", s_ln)
        s_rs = sop(lambda e: e.activation(RSTD[:, par:par + 1], LNV[:, par:par + 1],
                                          AF.Exp, scale=-0.5))
        wt("v", "s", s_rs)
        v_nm = vop(lambda e: e.tensor_scalar(NMR[:, par:par + 1], MV[:, par * 2:par * 2 + 1],
                                             RSTD[:, par:par + 1], -1.0,
                                             ALU.mult, ALU.mult))
        wt("s", "v", v_nm)
        wt("s", "t", lnt_free_T[par])
        s_ap = sop(lambda e, xpar=xpar: e.activation(
            LNT[par][:, :], XB[xpar][:, :], AF.Identity,
            scale=RSTD[:, par:par + 1], bias=NMR[:, par:par + 1]))
        apply_guard_S[par] = s_ap
        xb_free[xpar] = s_ap
        return s_ap

    def ln_transpose(par, s_apply, dst, col_fn):
        for half in range(2):
            slot = pr_n[0] % 2
            pr_n[0] += 1
            prb = PRB[slot]
            wt("t", "s", s_apply)
            wt("t", "s", pr_done[slot])
            t_grp = None
            for cc in range(4):
                c = half * 4 + cc
                t_grp = top(lambda e, c=c, cc=cc, prb=prb: e.matmul(
                    prb[:, cc * P:(cc + 1) * P],
                    LNT[par][:, c * P:(c + 1) * P], IDENT[:, :],
                    start=True, stop=True), inc=(cc == 3))
            wt("s", "t", t_grp)
            for cc in range(4):
                c = half * 4 + cc
                col = col_fn(c)
                if cc == 3:
                    pr_done[slot] = sop(lambda e, col=col, cc=cc, prb=prb: e.copy(
                        dst[:, col:col + P], prb[:, cc * P:(cc + 1) * P]))
                else:
                    sop(lambda e, col=col, cc=cc, prb=prb: e.copy(
                        dst[:, col:col + P], prb[:, cc * P:(cc + 1) * P]))
        lnt_free_T[par] = cT.v
        return pr_done[(pr_n[0] - 1) % 2]

    def proj_group(lhsT_fn, rhs_fn, bias_mm, evac_fn, wait_g=0, wait_s=0):
        slot = pr_n[0] % 2
        pr_n[0] += 1
        prb = PRB[slot]
        wt("t", "s", pr_done[slot])
        if wait_s:
            wt("t", "s", wait_s)
        if wait_g:
            wt("t", "g", wait_g)
        t_grp = None
        for c in range(8):
            last = (c == 7) and bias_mm is None
            t_grp = top(lambda e, c=c, prb=prb: e.matmul(
                prb[:, 0:512], lhsT_fn(c), rhs_fn(c),
                start=(c == 0), stop=(c == 7 and bias_mm is None)), inc=last)
        if bias_mm is not None:
            t_grp = top(lambda e, prb=prb: bias_mm(e, prb), inc=True)
        wt("s", "t", t_grp)
        pr_done[slot] = sop(lambda e, prb=prb: evac_fn(e, prb))
        return pr_done[slot]

    def dual_group(lhsT_fn, rhs_fn, bias_mm, evac_fn, wait_g=0, wait_s=0):
        slot = ds_n[0] % 3
        ds_n[0] += 1
        ba, bb = DSLOT[slot]
        wt("t", "s", ds_done[slot])
        if slot == 0:
            wt("t", "s", pr_done[0])
            wt("t", "s", pr_done[1])
        if wait_s:
            wt("t", "s", wait_s)
        if wait_g:
            wt("t", "g", wait_g)
        t_grp = None
        for c in range(8):
            for h in range(2):
                prb = ba if h == 0 else bb
                last = (c == 7 and h == 1) and bias_mm is None
                t_grp = top(lambda e, c=c, h=h, prb=prb: e.matmul(
                    prb[:, 0:512], lhsT_fn(c), rhs_fn(c, h),
                    start=(c == 0), stop=(c == 7 and bias_mm is None)), inc=last)
        if bias_mm is not None:
            top(lambda e, ba=ba: bias_mm(e, ba, 0), inc=False)
            t_grp = top(lambda e, bb=bb: bias_mm(e, bb, 1), inc=True)
        wt("s", "t", t_grp)
        sop(lambda e, ba=ba: evac_fn(e, ba, 0))
        ds_done[slot] = sop(lambda e, bb=bb: evac_fn(e, bb, 1))
        if slot == 0:
            pr_done[0] = ds_done[slot]
            pr_done[1] = ds_done[slot]
        return ds_done[slot]

    def pr_small(mm_fn, evac_fn, wv=0, ws=0):
        slot = pr_n[0] % 2
        pr_n[0] += 1
        prb = PRB[slot]
        wt("t", "s", pr_done[slot])
        if wv:
            wt("t", "v", wv)
        if ws:
            wt("t", "s", ws)
        t_b = top(lambda e, prb=prb: mm_fn(e, prb), inc=True)
        wt("s", "t", t_b)
        pr_done[slot] = sop(lambda e, prb=prb: evac_fn(e, prb))
        return pr_done[slot]

    wt("t", "g", g_smalls)
    q_done_qq = [0, 0, 0, 0]
    pre_done = {}

    def pre_A(t):
        par = t % 2
        xpar = t % 4
        xw = xload(xq[t * P:(t + 1) * P, :], xpar)
        return (par, ln_task(par, xw, xpar), t)

    def pre_B(st_):
        par, s_ap, t = st_
        return t, ln_transpose(par, s_ap, LNQF,
                               lambda c, t=t: 1024 * c + P * (t % 8))

    def q_proj(qq, s_lnq):
        for e_ in range(8):
            q_done_qq[qq] = proj_group(
                lambda c, e_=e_: WQB[:, c * D + e_ * P: c * D + (e_ + 1) * P],
                lambda c, qq=qq: LNQF[:, c * 1024 + (qq % 2) * 512:
                                      c * 1024 + (qq % 2) * 512 + 512],
                None,
                lambda e, prb, e_=e_, qq=qq: e.activation(
                    QF[:, e_ * 2048 + qq * 512: e_ * 2048 + (qq + 1) * 512],
                    prb[:, 0:512], AF.Identity, bias=BQ[:, e_:e_ + 1]),
                wait_g=g_aux_all, wait_s=s_lnq)

    stq = []
    xw8 = {}
    for t in range(8):
        stq.append(pre_A(t))
        if t >= 4:
            u = t + 4
            xw8[u] = xload(xq[u * P:(u + 1) * P, :], u % 4)
        if len(stq) >= 2:
            tt, s_l = pre_B(stq.pop(0))
            pre_done[tt] = s_l
            if tt == 3:
                q_proj(0, s_l)
    while stq:
        tt, s_l = pre_B(stq.pop(0))
        pre_done[tt] = s_l
    q_proj(1, pre_done[7])

    k01_done = 0
    for e_ in range(8):
        k01_done = dual_group(
            lambda c, e_=e_: WKB[:, c * D + e_ * P: c * D + (e_ + 1) * P],
            lambda c, h: LNQF[:, c * 1024 + h * 512: c * 1024 + h * 512 + 512],
            None,
            lambda e, prb, h, e_=e_: e.activation(
                QF[:, e_ * 2048 + 1024 + h * 512: e_ * 2048 + 1536 + h * 512],
                prb[:, 0:512], AF.Identity, bias=BK[:, e_:e_ + 1]),
            wait_g=g_aux_all, wait_s=pre_done[7])
    v01_done = 0
    for m in range(8):
        v01_done = dual_group(
            lambda c, m=m: LNQF[:, c * 1024 + m * P: c * 1024 + (m + 1) * P],
            lambda c, h: WVB[:, c * D + h * 512: c * D + h * 512 + 512],
            lambda e, prb, h: e.matmul(
                prb[:, 0:512], ONESC[:, :],
                BVROW[0:1, h * 512:(h + 1) * 512], start=False, stop=True),
            lambda e, prb, h, m=m: e.copy(
                FST[:, m * D + h * 512: m * D + h * 512 + 512], prb[:, 0:512]),
            wait_g=g_aux_all, wait_s=pre_done[m])

    cc_n = [0]
    ccKV_idx = [None, None, None, None]

    def emit_cc(inb, outb, ywait=None):
        wt("g", "st0", cST[0].v)
        wt("g", "st1", cST[1].v)
        key = "cc%d" % cc_n[0]
        cc_n[0] += 1

        def f(e, S):
            e.collective_compute(
                "AllGather", ALU.bypass,
                replica_groups=[[0, 1, 2, 3], [4, 5, 6, 7]],
                ins=[inb.ap()], outs=[outb.ap()]).then_inc(S[key], 1)
        ops["g"].append(f)
        return key

    wt("y", "s", k01_done)
    yKout = [0, 0, 0, 0]
    yVout = [0, 0, 0, 0]
    for G in range(2):
        wt("y", "st0", cST[0].v)
        ko = (ccK0_in.ap() if G == 0 else ccKV_in[G].ap()[0:1024, :])
        yKout[G] = ydma(lambda e, G=G, ko=ko: e.dma_start(
            out=ko.rearrange("(p e) q -> p e q", p=P),
            in_=QF[:, :].rearrange("p (e q) -> p e q", e=8)[
                :, :, 1024 + G * 512: 1536 + G * 512]), "st0")
    ccK0_key = emit_cc(ccK0_in, ccK0_out)
    wt("y", "s", v01_done)
    for G in range(2):
        wt("y", "st1", cST[1].v)
        vo = (ccV0_in.ap() if G == 0 else ccKV_in[G].ap()[1024:2048, :])
        yVout[G] = ydma(lambda e, G=G, vo=vo: e.dma_start(
            out=vo.rearrange("(q t e2) c -> t q (e2 c)", q=4, e2=2),
            in_=FST[:, G * 4096:(G + 1) * 4096].rearrange(
                "t (q emb) -> t q emb", q=4)), "st1")


    ccV0_key = emit_cc(ccV0_in, ccV0_out)
    ccKV_idx[1] = emit_cc(ccKV_in[1], ccKV_out[1])

    def pre_A2(t):
        par = t % 2
        xpar = t % 4
        xw = xw8[t] if t in xw8 else xload(xq[t * P:(t + 1) * P, :], xpar)
        return (par, ln_task(par, xw, xpar), t)

    stq = []
    for t in range(8, 16):
        stq.append(pre_A2(t))
        if len(stq) >= 2:
            tt, s_l = pre_B(stq.pop(0))
            pre_done[tt] = s_l
            if tt == 11:
                wt("s", "st0", cST[0].v)
                q_proj(2, s_l)
            if tt == 13:
                vg2_done = 0
                for m in range(8, 12):
                    vg2_done = dual_group(
                        lambda c, m=m: LNQF[:, c * 1024 + (m % 8) * P:
                                            c * 1024 + (m % 8 + 1) * P],
                        lambda c, h: WVB[:, c * D + h * 512: c * D + h * 512 + 512],
                        lambda e, prb, h: e.matmul(
                            prb[:, 0:512], ONESC[:, :],
                            BVROW[0:1, h * 512:(h + 1) * 512], start=False, stop=True),
                        lambda e, prb, h, m=m: e.copy(
                            VSTG[:, (m - 8) * D + h * 512: (m - 8) * D + h * 512 + 512],
                            prb[:, 0:512]),
                        wait_g=g_aux_all, wait_s=pre_done[m])
                wt("y", "s", vg2_done)
                wt("y", "st1", cST[1].v)
                yVout[2] = ydma(lambda e: e.dma_start(
                    out=ccKV_in[2].ap()[1024:2048, :].rearrange(
                        "(q t e2) c -> t q (e2 c)", q=4, e2=2),
                    in_=VSTG[:, :].rearrange("t (q emb) -> t q emb", q=4)), "st1")
    while stq:
        tt, s_l = pre_B(stq.pop(0))
        pre_done[tt] = s_l

    wt("s", "st1", cST[1].v)
    k23_done = 0
    for e_ in range(8):
        k23_done = dual_group(
            lambda c, e_=e_: WKB[:, c * D + e_ * P: c * D + (e_ + 1) * P],
            lambda c, h: LNQF[:, c * 1024 + h * 512: c * 1024 + h * 512 + 512],
            None,
            lambda e, prb, h, e_=e_: e.activation(
                FST[:, e_ * D + h * 512: e_ * D + h * 512 + 512],
                prb[:, 0:512], AF.Identity, bias=BK[:, e_:e_ + 1]),
            wait_g=g_aux_all, wait_s=pre_done[15])
    t_after_K23 = cT.v
    q_proj(3, pre_done[15])
    t_after_Q23 = cT.v
    wt("y", "s", k23_done)
    for G in range(2, 4):
        wt("y", "st0", cST[0].v)
        yKout[G] = ydma(lambda e, G=G: e.dma_start(
            out=ccKV_in[G].ap()[0:1024, :].rearrange("(p e) q -> p e q", p=P),
            in_=FST[:, :].rearrange("p (e q) -> p e q", e=8)[
                :, :, (G - 2) * 512: (G - 1) * 512]), "st0")
    ccKV_idx[2] = emit_cc(ccKV_in[2], ccKV_out[2])

    wt("s", "st1", cST[1].v)
    vg3_done = 0
    for m in range(12, 16):
        vg3_done = dual_group(
            lambda c, m=m: LNQF[:, c * 1024 + (m % 8) * P:
                                c * 1024 + (m % 8 + 1) * P],
            lambda c, h: WVB[:, c * D + h * 512: c * D + h * 512 + 512],
            lambda e, prb, h: e.matmul(
                prb[:, 0:512], ONESC[:, :],
                BVROW[0:1, h * 512:(h + 1) * 512], start=False, stop=True),
            lambda e, prb, h, m=m: e.copy(
                VSTG[:, (m - 12) * D + h * 512: (m - 12) * D + h * 512 + 512],
                prb[:, 0:512]),
            wait_g=g_aux_all, wait_s=pre_done[m])
    t_after_V3 = cT.v
    wt("y", "s", vg3_done)
    wt("y", "st1", cST[1].v)
    yVout[3] = ydma(lambda e: e.dma_start(
        out=ccKV_in[3].ap()[1024:2048, :].rearrange(
            "(q t e2) c -> t q (e2 c)", q=4, e2=2),
        in_=VSTG[:, :].rearrange("t (q emb) -> t q emb", q=4)), "st1")
    ccKV_idx[3] = emit_cc(ccKV_in[3], ccKV_out[3])
    pre_end_S = cS.v

    def land_K_pair(G, wait_half=(0, 0)):
        wt("y", ccK0_key if G == 0 else ccKV_idx[G], 1)
        for half in range(2):
            wt("y", "t", wait_half[half])
            for rr in range(2):
                r = half * 2 + rr
                if G % 2 == 0:
                    buf = KPB_lo if half == 0 else KPB_hi
                    dst = buf[:, rr * 4096:(rr + 1) * 4096]
                else:
                    dst = KPA[:, r * 4096:(r + 1) * 4096]
                ksrc = (ccK0_out.ap() if G == 0 else ccKV_out[G].ap())
                kstep = 1024 if G == 0 else 2048
                src = ksrc[r * kstep:r * kstep + 1024, :].rearrange(
                    "(p e) q -> p (e q)", p=P)
                ydma(lambda e, dst=dst, src=src: e.dma_start(out=dst, in_=src),
                     "kl%d" % (G % 2))
        return cKL[G % 2].v

    def land_V(pb):
        G, b = pb // 2, pb % 2
        buf = VBUF[pb % 2]
        wt("g", ccV0_key if G == 0 else ccKV_idx[G], 1)
        for r in range(4):
            def f(e, S, G=G, b=b, r=r, buf=buf, pb=pb):
                if G == 0:
                    vsrc, base = ccV0_out.ap(), r * 1024 + b * 512
                else:
                    vsrc, base = ccKV_out[G].ap(), r * 2048 + 1024 + b * 512
                e.dma_start(
                    out=buf[:, r * 2048:(r + 1) * 2048].rearrange(
                        "t (hp emb) -> t hp emb", hp=2),
                    in_=vsrc[base: base + 512, :].rearrange(
                        "(hp t e2) c -> t hp (e2 c)", hp=2, e2=2)
                ).then_inc(S["vl%d" % (pb % 2)], 16)
            ops["g"].append(f)
            cVL[pb % 2].inc(16)
        return cVL[pb % 2].v

    land_K_pair(0, (t_after_K23, t_after_Q23))
    wt("g", "st0", cST[0].v)
    wt("g", "st1", cST[1].v)
    land_V(0)
    wt("y", "t", t_after_V3)
    wt("y", "s", xb_free[2])
    wt("y", "s", xb_free[3])
    wt("y", "st1", cST[1].v)
    land_K_pair(1)
    wt("g", "t", t_after_V3)
    land_V(1)

    exp_done_S = {}
    v3_done_V = {}
    evac_done_S = {}
    t_T = {}
    stt_done_V = {}
    sgrp_done_T = {}
    seen_S = set()
    seen_AV = set()
    phase_S_end = {}
    phase_AV_end = {}

    def emit_S(g, p, i):
        par = g % 2
        q4 = g % 4
        diag_a = (i == 2 * p)
        diag_b = (i == 2 * p + 1)
        diag = diag_a or diag_b
        if g >= 2:
            wt("t", "s", evac_done_S.get(g - 2, 0))
        else:
            wt("t", "s", pre_end_S)
        if p not in seen_S:
            seen_S.add(p)
            GG = p // 2
            wt("t", "kl%d" % (GG % 2), 64 * (GG // 2 + 1))
        if p == 0:
            wt("t", "s", q_done_qq[i // 4])
        G, b = p // 2, p % 2

        def k_rhs(c, h):
            if G % 2 == 0:
                buf = KPB_lo if h == 0 else KPB_hi
                return buf[:, :].rearrange("p (rr cc bb q) -> p rr cc bb q",
                                           rr=2, cc=8, bb=2)[:, :, c, b]
            return KPA[:, :].rearrange("p (r cc bb q) -> p r cc bb q",
                                       r=4, cc=8, bb=2)[:, 2 * h:2 * h + 2, c, b]

        t_last = None
        for c in range(7):
            for h in range(2):
                top(lambda e, c=c, h=h, par=par, i=i, rhs=k_rhs(c, h): e.matmul(
                    PS_S[par][:, h * 512:(h + 1) * 512],
                    QF[:, c * 2048 + i * P: c * 2048 + (i + 1) * P],
                    rhs, start=(c == 0), stop=False))
        if diag_a:
            for h in range(2):
                top(lambda e, h=h, par=par: e.matmul(
                    PS_S[par][:, h * 512:(h + 1) * 512], IDENT[:, :],
                    MASKA[:, h * 512:(h + 1) * 512], start=False, stop=False))
        elif diag_b:
            for r in range(4):
                top(lambda e, r=r, par=par: e.matmul(
                    PS_S[par][:, r * 256 + 128: r * 256 + 256], IDENT[:, :],
                    MASKA[:, r * 256: r * 256 + 128], start=False, stop=False))
        for h in range(2):
            t_last = top(lambda e, h=h, par=par, i=i, rhs=k_rhs(7, h): e.matmul(
                PS_S[par][:, h * 512:(h + 1) * 512],
                QF[:, 7 * 2048 + i * P: 7 * 2048 + (i + 1) * P],
                rhs, start=False, stop=True), inc=(h == 1))
        sgrp_done_T[g] = t_last
        if i == 15:
            phase_S_end[p] = cT.v
        wt("v", "t", t_last)
        v1 = vop(lambda e, par=par, q4=q4: e.tensor_reduce(
            NMAX[:, q4:q4 + 1], PS_S[par][:, :], AX.X, ALU.max, negate=True))
        wt("v", "v", v1)
        v2 = vop(lambda e, q4=q4, i=i: e.tensor_scalar(
            NMS[:, q4:q4 + 1], NMAX[:, q4:q4 + 1], 1.0 / 32.0,
            NMRUN[:, i:i + 1], ALU.mult, ALU.min))
        wt("v", "v", v2)
        v3 = vop(lambda e, q4=q4, i=i: e.tensor_tensor(
            DSC[:, q4:q4 + 1], NMS[:, q4:q4 + 1], NMRUN[:, i:i + 1], ALU.subtract))
        wt("v", "v", v3)
        vop(lambda e, q4=q4, i=i: e.tensor_copy(NMRUN[:, i:i + 1], NMS[:, q4:q4 + 1]))
        v3_done_V[g] = v3

    def emit_T(g, p, i):
        par = g % 2
        q4 = g % 4
        wt("s", "v", v3_done_V[g])
        wt("s", "t", t_T.get(g - 1, 0))
        s2 = sop(lambda e, par=par, q4=q4: e.activation(
            EB[:, :], PS_S[par][:, :], AF.Exp, scale=1.0 / 32.0,
            bias=NMS[:, q4:q4 + 1], accum_out=ACC[:, q4:q4 + 1]))
        exp_done_S[g] = s2
        wt("t", "s", exp_done_S[g])
        tl = None
        for e_ in range(8):
            tl = top(lambda e, e_=e_, par=par: e.matmul(
                PS_S[par][:, e_ * P:(e_ + 1) * P], EB[:, e_ * P:(e_ + 1) * P],
                IDENT[:, :], start=True, stop=True), inc=(e_ == 7))
        t_T[g] = tl
        wt("s", "t", tl)
        evac_done_S[g] = sop(lambda e, par=par, g=g: e.copy(
            ETS[g % 2][:, :], PS_S[par][:, :]))

    def emit_AV(g, p, i):
        q4 = g % 4
        wt("s", "v", v3_done_V[g])
        s1 = sop(lambda e, q4=q4: e.activation(
            ALS[:, q4:q4 + 1], DSC[:, q4:q4 + 1], AF.Exp))
        wt("t", "s", evac_done_S[g])
        wt("t", "v", stt_done_V.get(g - 1, 0))
        if p not in seen_AV:
            seen_AV.add(p)
            wt("t", "vl%d" % (p % 2), 64 * (p // 2 + 1))
        VB = VBUF[p % 2]
        avb = None
        for e_ in range(8):
            for j in range(2):
                avb = top(lambda e, e_=e_, j=j, g=g, VB=VB: e.matmul(
                    PS_AV[:, j * 512:(j + 1) * 512],
                    ETS[g % 2][:, e_ * P:(e_ + 1) * P],
                    VB[:, e_ * D + j * 512: e_ * D + j * 512 + 512],
                    start=(e_ == 0), stop=(e_ == 7)),
                    inc=(e_ == 7 and j == 1))
        wt("v", "s", s1)
        wt("v", "s", exp_done_S[g])
        vop(lambda e, q4=q4, i=i: e.scalar_tensor_tensor(
            LRUN[:, i:i + 1], LRUN[:, i:i + 1], ALS[:, q4:q4 + 1],
            ACC[:, q4:q4 + 1], ALU.mult, ALU.add))
        wt("v", "t", avb)
        stt_done_V[g] = vop(lambda e, q4=q4, i=i: e.scalar_tensor_tensor(
            O[:, i * D:(i + 1) * D], O[:, i * D:(i + 1) * D], ALS[:, q4:q4 + 1],
            PS_AV[:, :], ALU.mult, ALU.add))
        if i == 15:
            phase_AV_end[p] = cT.v
            if p in (1, 3):
                wt("y", "t", phase_S_end[p])
                land_K_pair(p // 2 + 2)
            if p <= 5:
                wt("g", "t", phase_AV_end[p])
                land_V(p + 2)

    g = 0
    tq = []
    aq = []
    for p in range(NBLK):
        for i in range(2 * p, NTILE):
            emit_S(g, p, i)
            tq.append((g, p, i))
            if len(tq) >= 2:
                u = tq.pop(0)
                emit_T(*u)
                aq.append(u)
            if len(aq) >= 2:
                emit_AV(*aq.pop(0))
            g += 1
    while tq or aq:
        if tq:
            u = tq.pop(0)
            emit_T(*u)
            aq.append(u)
        elif aq:
            emit_AV(*aq.pop(0))

    last_stt_V = stt_done_V[g - 1]
    last_sgrp_T = sgrp_done_T[g - 1]
    last_avb_T = cT.v
    last_tT = t_T[g - 1]

    wt("v", "v", cV.v)
    v_nmin = vop(lambda e: e.tensor_reduce(NMIN[:, :], NMRUN[:, :], AX.X, ALU.min))
    wt("s", "v", v_nmin)
    s_nb = sop(lambda e: e.copy(NMINB[:, :], NMIN[:, :]))
    wt("s", "t", last_sgrp_T)
    s_srow = pr_small(lambda e, prb: e.matmul(prb[0:1, 0:P], NMINB[:, :],
                                              IDENT[:, :], start=True, stop=True),
                      lambda e, prb: e.copy(SROW[:, :], prb[0:1, 0:P]), ws=s_nb)
    wt("v", "s", s_srow)
    v_nmg = vop(lambda e: e.tensor_reduce(MG[:, :], SROW[:, :], AX.X, ALU.min))
    s_b1 = pr_small(lambda e, prb: e.matmul(prb[:, 0:1], ONESR[:, :], MG[:, :],
                                            start=True, stop=True),
                    lambda e, prb: e.copy(DLB[:, :], prb[:, 0:1]), wv=v_nmg)
    wt("s", "s", s_b1)
    s_exs = sop(lambda e: e.activation(EXS[:, :], NMRUN[:, :], AF.Exp, scale=-1.0,
                                       bias=DLB[:, 0:1]))
    wt("v", "s", s_exs)
    v_z = vop(lambda e: e.tensor_tensor(EXS[:, :], EXS[:, :], LRUN[:, :], ALU.mult))
    wt("v", "v", v_z)
    v_zc = vop(lambda e: e.tensor_reduce(ZC[:, :], EXS[:, :], AX.X, ALU.add))
    s_sc2 = pr_small(lambda e, prb: e.matmul(prb[0:1, 0:1], ZC[:, :], ONESF[:, :],
                                             start=True, stop=True),
                     lambda e, prb: e.copy(SCP[0:1, 1:2], prb[0:1, 0:1]), wv=v_zc)
    v_mc = vop(lambda e: e.tensor_scalar(SCP[0:1, 0:1], MG[0:1, 0:1], -1.0, None,
                                         ALU.mult))
    wt("g", "v", v_mc)
    wt("g", "s", s_sc2)
    g_pack = gdma(lambda e: e.dma_start(out=cc_in.ap(), in_=SCP[:, :]))
    gbar()

    cc_stats = emit_cc(cc_in, cc_out, 0)

    wst_free_V = [0, 0]
    for j in range(8):
        par = j % 2
        wt("y", "t", last_avb_T)
        wt("y", "v", wst_free_V[par])
        g_wj = ydma(lambda e, j=j, par=par: e.dma_start(
            out=WST[par][:, :], in_=wt_d[j * P:(j + 1) * P, :]), "x%d" % par)
        wt("v", "x%d" % par, g_wj)
        wst_free_V[par] = vop(lambda e, j=j, par=par: e.tensor_reduce(
            ABSA[:, j:j + 1], WST[par][:, :], AX.X, ALU.add,
            apply_absolute_value=True))
    wt("v", "v", cV.v)
    v_abs2 = vop(lambda e: e.tensor_reduce(NMIN[:, :], ABSA[:, :], AX.X, ALU.add))
    s_dlt = pr_small(lambda e, prb: e.matmul(prb[0:1, 0:1], NMIN[:, :], ONESF[:, :],
                                             start=True, stop=True),
                     lambda e, prb: e.activation(DLT[:, :], prb[0:1, 0:1], AF.Identity,
                                                 scale=0.7 / (1024.0 * 1024.0)),
                     wv=v_abs2)
    s_db = pr_small(lambda e, prb: e.matmul(prb[:, 0:1], ONESR[:, :], DLT[:, :],
                                            start=True, stop=True),
                    lambda e, prb: e.copy(DLB[:, :], prb[:, 0:1]), ws=s_dlt)
    wt("v", "s", s_db)
    v_ndb = vop(lambda e: e.tensor_scalar(NDLB[:, :], DLB[:, :], -1.0, None, ALU.mult))
    stil_free_T = [0]
    v_st2 = 0
    for j in range(8):
        par = j % 2
        wt("y", "v", wst_free_V[par])
        g_wj = ydma(lambda e, j=j, par=par: e.dma_start(
            out=WST[par][:, :], in_=wt_d[j * P:(j + 1) * P, :]), "x%d" % par)
        wt("v", "x%d" % par, g_wj)
        wt("v", "v", v_ndb)
        if j > 0:
            wt("v", "v", v_st2)
        vop(lambda e, par=par: e.tensor_scalar(
            GTA[:, :], WST[par][:, :], DLB[:, 0:1], None, ALU.is_gt))
        v_cmp = vop(lambda e, par=par: e.tensor_scalar(
            GTB[:, :], WST[par][:, :], NDLB[:, 0:1], None, ALU.is_lt))
        wst_free_V[par] = v_cmp
        wt("v", "v", v_cmp)
        v_sub = vop(lambda e: e.tensor_tensor(GTA[:, :], GTA[:, :], GTB[:, :],
                                              ALU.subtract))
        wt("v", "v", v_sub)
        wt("v", "t", stil_free_T[0])
        v_st2 = vop(lambda e: e.tensor_scalar(STIL[:, :], GTA[:, :], 2.0, None,
                                              ALU.mult))
        for half in range(2):
            slot = pr_n[0] % 2
            pr_n[0] += 1
            prb = PRB[slot]
            wt("t", "v", v_st2)
            wt("t", "s", pr_done[slot])
            t_g2 = None
            for cc in range(4):
                c = half * 4 + cc
                t_g2 = top(lambda e, c=c, cc=cc, prb=prb: e.matmul(
                    prb[:, cc * P:(cc + 1) * P], STIL[:, c * P:(c + 1) * P],
                    IDENT[:, :], start=True, stop=True), inc=(cc == 3))
            wt("s", "t", t_g2)
            for cc in range(4):
                c = half * 4 + cc
                if cc == 3:
                    pr_done[slot] = sop(lambda e, c=c, cc=cc, j=j, prb=prb: e.copy(
                        WTB[:, c * D + j * P: c * D + (j + 1) * P],
                        prb[:, cc * P:(cc + 1) * P]))
                else:
                    sop(lambda e, c=c, cc=cc, j=j, prb=prb: e.copy(
                        WTB[:, c * D + j * P: c * D + (j + 1) * P],
                        prb[:, cc * P:(cc + 1) * P]))
        stil_free_T[0] = cT.v
    s_wtb = pr_done[(pr_n[0] - 1) % 2]
    g_al = gdma(lambda e: e.dma_start(out=ALV[:, :], in_=al_in))
    wt("g", "t", last_avb_T)
    g_fb = gdma(lambda e: e.dma_start(out=FBROW[:, :], in_=fb_in))
    gbar()
    wt("v", "g", g_fb)
    v_ia = vop(lambda e: e.reciprocal(T1[:, :], ALV[:, :]))
    wt("v", "v", v_ia)
    v_i2 = vop(lambda e: e.tensor_scalar(T1[:, :], T1[:, :], 2.0, None, ALU.mult))
    wt("v", "v", v_i2)
    wt("v", "t", last_tT)
    v_bs = vop(lambda e: e.tensor_scalar(BSROW[:, :], FBROW[:, :], T1[0:1, 0:1],
                                         None, ALU.mult))
    v_ah = vop(lambda e: e.tensor_scalar(PAIR[0:1, 0:1], ALV[0:1, 0:1], 0.5, None,
                                         ALU.mult))
    s_ahb = pr_small(lambda e, prb: e.matmul(prb[:, 0:1], ONESR[:, :], PAIR[0:1, 0:1],
                                             start=True, stop=True),
                     lambda e, prb: e.copy(AHB[:, :], prb[:, 0:1]), wv=v_ah)

    wt("g", cc_stats, 1)
    g_g8 = gdma(lambda e: e.dma_start(
        out=G8[:, :], in_=cc_out.ap().rearrange("(a r) t -> a (r t)", a=1)))
    gbar()
    wt("v", "g", g_g8)
    v_mg = vop(lambda e: e.tensor_reduce(MG[:, :], G8[:, 0:8:2], AX.X, ALU.max))
    wt("v", "v", v_mg)
    v_ng2 = vop(lambda e: e.tensor_scalar(NMG2[:, :], MG[:, :], -1.0, None, ALU.mult))
    wt("s", "v", v_ng2)
    s_er = sop(lambda e: e.activation(ER4[:, :], G8[:, 0:8:2], AF.Exp,
                                      bias=NMG2[0:1, 0:1]))
    wt("v", "s", s_er)
    v_zr = vop(lambda e: e.tensor_tensor(ZR4[:, :], ER4[:, :], G8[:, 1:8:2], ALU.mult))
    wt("v", "v", v_zr)
    v_zs = vop(lambda e: e.tensor_reduce(ZS[:, :], ZR4[:, :], AX.X, ALU.add))
    wt("s", "v", v_zs)
    s_lnz = sop(lambda e: e.activation(IVZ[:, :], ZS[:, :], AF.Ln))
    wt("v", "s", s_lnz)
    v_pair = vop(lambda e: e.tensor_tensor(PAIR[0:1, 0:1], NMG2[0:1, 0:1],
                                           IVZ[0:1, 0:1], ALU.subtract))
    s_fb2 = pr_small(lambda e, prb: e.matmul(prb[:, 0:1], ONESR[:, :], PAIR[0:1, 0:1],
                                             start=True, stop=True),
                     lambda e, prb: e.copy(FB2[:, 0:1], prb[:, 0:1]), wv=v_pair)

    ost_free_ST = [0, 0]
    x2_done_V = {}
    FSLOT = [PS_AV, PS_S[0], PS_S[1]]
    fs_done = [("v", last_stt_V),
               ("s", evac_done_S.get(g - 2, 0)),
               ("s", evac_done_S.get(g - 1, 0))]
    apply_S = {}
    mv_guard = [apply_guard_S[0], apply_guard_S[1]]
    rstd_guard = [0, 0]
    f2_S = {}
    aggr_V = {}
    ffn_T = {}
    lnfm_S = {}

    def ffn_a(i):
        par = i % 2
        wt("y", "v", x2_done_V.get(i - 2, 0))
        if i < 2:
            wt("y", "st1", yVout[3])
            wt("y", "v", wst_free_V[par])
        xw = ydma(lambda e, i=i, par=par: e.dma_start(
            out=XB[par][:, :], in_=xq[i * P:(i + 1) * P, :]), "x%d" % par)
        wt("s", "s", s_fb2)
        wt("s", "v", x2_done_V.get(i - 2, 0))
        s_f = sop(lambda e, i=i, par=par: e.activation(
            F2[:, par:par + 1], NMRUN[:, i:i + 1], AF.Exp, scale=-1.0,
            bias=FB2[:, 0:1]))
        f2_S[i] = s_f
        wt("v", "s", s_f)
        wt("v", "x%d" % par, xw)
        v_x2 = vop(lambda e, i=i, par=par: e.scalar_tensor_tensor(
            O[:, i * D:(i + 1) * D], O[:, i * D:(i + 1) * D], F2[:, par:par + 1],
            XB[par][:, :], ALU.mult, ALU.add))
        x2_done_V[i] = v_x2
        wt("v", "s", mv_guard[par])
        wt("v", "v", v_x2)
        vop(lambda e, i=i, par=par: e.bn_stats(
            BN6[:, par * 12:par * 12 + 6], O[:, i * D:i * D + 512]))
        v_bn = vop(lambda e, i=i, par=par: e.bn_stats(
            BN6[:, par * 12 + 6:par * 12 + 12], O[:, i * D + 512:(i + 1) * D]))
        wt("v", "v", v_bn)
        aggr_V[i] = vop(lambda e, par=par: e.bn_aggr(
            MV[:, par * 2:par * 2 + 2], BN6[:, par * 12:par * 12 + 12]))

    def ffn_b(i):
        par = i % 2
        wt("s", "v", aggr_V[i])
        s_ln = sop(lambda e, par=par: e.activation(
            LNV[:, par:par + 1], MV[:, par * 2 + 1:par * 2 + 2], AF.Ln,
            bias=EPSC[:, 0:1]))
        mv_guard[par] = s_ln
        wt("s", "s", s_ln)
        wt("s", "v", rstd_guard[par])
        s_rs = sop(lambda e, par=par: e.activation(
            RSTD[:, par:par + 1], LNV[:, par:par + 1], AF.Exp, scale=-0.5))
        wt("v", "s", s_rs)
        wt("v", "t", lnt_free_T[par])
        v_ap = vop(lambda e, i=i, par=par: e.tensor_scalar(
            LNT[par][:, :], O[:, i * D:(i + 1) * D], MV[:, par * 2:par * 2 + 1],
            RSTD[:, par:par + 1], ALU.subtract, ALU.mult))
        rstd_guard[par] = v_ap
        apply_S[i] = v_ap

    def ffn_c(i):
        par = i % 2
        slot = i % 3
        for half in range(2):
            pslot = pr_n[0] % 2
            pr_n[0] += 1
            prb = PRB[pslot]
            wt("t", "v", apply_S[i])
            wt("t", "s", pr_done[pslot])
            t_grp = None
            for cc in range(4):
                c = half * 4 + cc
                t_grp = top(lambda e, c=c, cc=cc, prb=prb, par=par: e.matmul(
                    prb[:, cc * P:(cc + 1) * P],
                    LNT[par][:, c * P:(c + 1) * P], IDENT[:, :],
                    start=True, stop=True), inc=(cc == 3))
            wt("s", "t", t_grp)
            for cc in range(4):
                c = half * 4 + cc
                if cc == 3:
                    pr_done[pslot] = sop(lambda e, c=c, cc=cc, prb=prb: e.copy(
                        LN2F[:, c * P:(c + 1) * P], prb[:, cc * P:(cc + 1) * P]))
                else:
                    sop(lambda e, c=c, cc=cc, prb=prb: e.copy(
                        LN2F[:, c * P:(c + 1) * P], prb[:, cc * P:(cc + 1) * P]))
        lnt_free_T[par] = cT.v
        lnfm_S[i] = pr_done[(pr_n[0] - 1) % 2]
        wt("t", "s", lnfm_S[i])
        if i == 0:
            wt("t", "s", s_wtb)
            wt("t", "v", v_bs)
        wt("t", fs_done[slot][0], fs_done[slot][1])
        t_ffn = None
        for c in range(8):
            for j in range(2):
                top(lambda e, c=c, j=j, slot=slot: e.matmul(
                    FSLOT[slot][:, j * 512:(j + 1) * 512], LN2F[:, c * P:(c + 1) * P],
                    WTB[:, c * D + j * 512: c * D + j * 512 + 512],
                    start=(c == 0), stop=False))
        for j in range(2):
            t_ffn = top(lambda e, j=j, slot=slot: e.matmul(
                FSLOT[slot][:, j * 512:(j + 1) * 512], ONESC[:, :],
                BSROW[0:1, j * 512:(j + 1) * 512], start=False, stop=True),
                inc=(j == 1))
        ffn_T[i] = t_ffn

    def ffn_d(i):
        par = i % 2
        slot = i % 3
        wt("v", "t", ffn_T[i])
        wt("v", "s", s_ahb)
        wt("v", "st%d" % par, ost_free_ST[par])
        ep = vop(lambda e, i=i, par=par, slot=slot: e.scalar_tensor_tensor(
            OST[par][:, :], FSLOT[slot][:, :], AHB[:, 0:1], O[:, i * D:(i + 1) * D],
            ALU.mult, ALU.add))
        fs_done[slot] = ("v", ep)
        wt("y", "v", ep)
        st_val = ydma(lambda e, i=i, par=par: e.dma_start(
            out=y[i * P:(i + 1) * P, :], in_=OST[par][:, :]), "st%d" % par)
        ost_free_ST[par] = st_val

    for i in range(NTILE + 3):
        if i < NTILE:
            ffn_a(i)
        if 1 <= i < NTILE + 1:
            ffn_b(i - 1)
        if 2 <= i < NTILE + 2:
            ffn_c(i - 2)
        if i >= 3:
            ffn_d(i - 3)

    wt("y", "st0", cST[0].v)
    wt("y", "st1", cST[1].v)

    with (
        nc.semaphore("sem_t") as sem_t,
        nc.semaphore("sem_s") as sem_s,
        nc.semaphore("sem_v") as sem_v,
        nc.semaphore("sem_x0") as sem_x0,
        nc.semaphore("sem_x1") as sem_x1,
        nc.semaphore("sem_x2") as sem_x2,
        nc.semaphore("sem_x3") as sem_x3,
        nc.semaphore("sem_g") as sem_g,
        nc.semaphore("sem_st0") as sem_st0,
        nc.semaphore("sem_st1") as sem_st1,
        nc.semaphore("sem_kl0") as sem_kl0,
        nc.semaphore("sem_kl1") as sem_kl1,
        nc.semaphore("sem_vl0") as sem_vl0,
        nc.semaphore("sem_vl1") as sem_vl1,
        nc.Block() as block,
    ):
        S = {"t": sem_t, "s": sem_s, "v": sem_v, "x0": sem_x0, "x1": sem_x1,
             "x2": sem_x2, "x3": sem_x3,
             "g": sem_g, "st0": sem_st0, "st1": sem_st1,
             "kl0": sem_kl0, "kl1": sem_kl1, "vl0": sem_vl0, "vl1": sem_vl1}
        for ci in range(cc_n[0]):
            S["cc%d" % ci] = nc.alloc_semaphore(name="sem_cc%d" % ci)

        @block.tensor
        def _(t):
            for f in ops["t"]:
                f(t, S)

        @block.scalar
        def _(s):
            for f in ops["s"]:
                f(s, S)

        @block.vector
        def _(v):
            for f in ops["v"]:
                f(v, S)

        @block.gpsimd
        def _(gp):
            for f in ops["g"]:
                f(gp, S)

        @block.sync
        def _(sy):
            for f in ops["y"]:
                f(sy, S)

    return nc


_NC_CACHE = None


def _get_nc():
    global _NC_CACHE
    if _NC_CACHE is None:
        _NC_CACHE = build()
    return _NC_CACHE


def kernel(L0_attn_x, L0_attn_Wq, L0_attn_bq, L0_attn_Wk, L0_attn_bk,
           L0_attn_Wv, L0_attn_bv, L0_ffn_weight, L0_ffn_alpha, L0_ffn_bias):
    x = np.ascontiguousarray(np.asarray(L0_attn_x, dtype=np.float32))
    wq_h = np.ascontiguousarray(np.asarray(L0_attn_Wq, dtype=np.float32))
    wk_h = np.ascontiguousarray(np.asarray(L0_attn_Wk, dtype=np.float32))
    wv_h = np.ascontiguousarray(np.asarray(L0_attn_Wv, dtype=np.float32))
    wt_h = np.ascontiguousarray(np.asarray(L0_ffn_weight, dtype=np.float32))
    bq_h = np.ascontiguousarray(np.asarray(L0_attn_bq, dtype=np.float32).reshape(8, P).T)
    bk_h = np.ascontiguousarray(np.asarray(L0_attn_bk, dtype=np.float32).reshape(8, P).T)
    bv_h = np.ascontiguousarray(np.asarray(L0_attn_bv, dtype=np.float32).reshape(1, D))
    fb_h = np.ascontiguousarray(np.asarray(L0_ffn_bias, dtype=np.float32).reshape(1, D))
    al_h = np.ascontiguousarray(np.asarray(L0_ffn_alpha, dtype=np.float32).reshape(1, 1))
    ident = np.eye(P, dtype=np.float32).astype(ml_dtypes.bfloat16)
    onescol = np.ones((1, P), dtype=np.float32).astype(ml_dtypes.bfloat16)

    r_ = np.arange(P)[:, None]
    colv = np.arange(D)[None, :]
    rk = colv // 256
    hp = (colv // 128) % 2
    tt = colv % 128
    mt = 4 * hp + rk

    in_maps = []
    for core in range(8):
        b = core // 4
        c = core % 4
        xb_full = x[b]
        xq_h = np.ascontiguousarray(xb_full.reshape(64, P, D)[c::4].reshape(2048, D))
        allow_a = (mt < c) | ((mt == c) & (tt <= r_))
        maska = np.where(allow_a, 0.0, -1e30).astype(np.float32)
        in_maps.append({
            "xq": xq_h,
            "wq": wq_h, "wk": wk_h, "wv": wv_h, "wt": wt_h,
            "bq": bq_h, "bk": bk_h, "bv": bv_h, "fb": fb_h, "alpha": al_h,
            "maska": maska.astype(ml_dtypes.bfloat16),
            "ident": ident, "onescol": onescol,
        })

    nc = _get_nc()
    res = run_bass_kernel_spmd(nc, in_maps, core_ids=list(range(8)))

    out = np.empty((2, T, D), dtype=np.float32)
    for core in range(8):
        b = core // 4
        c = core % 4
        yc = np.asarray(res.results[core]["y"], dtype=np.float32)
        out[b].reshape(64, P, D)[c::4] = yc.reshape(NTILE, P, D)
    return out
